# version 14
# baseline (speedup 1.0000x reference)
import numpy as np
import ml_dtypes
import concourse.bass as bass
import concourse.mybir as mybir
from concourse.bass_utils import run_bass_kernel_spmd
from contextlib import ExitStack

F32 = mybir.dt.float32
BF16 = mybir.dt.bfloat16
AX = mybir.AxisListType
ALU = mybir.AluOpType
ACT = mybir.ActivationFunctionType

R = 2048
NCH = 16
D = 1024
NK = 8
H = 8
I2 = 4096
I3 = 2816
EPS = 1e-5

ENGINES = ("sync", "gpsimd", "vector", "scalar", "tensor")


class KB:
    def __init__(self):
        self.evs = {e: [] for e in ENGINES}
        self.cnt = {}

    def ev(self, eng, fn, waits=(), inc=None, mode="inst"):
        w = [(s, v) for (s, v) in waits if v and v > 0]
        self.evs[eng].append((w, fn, inc, mode))
        if inc is None:
            return None
        self.cnt[inc] = self.cnt.get(inc, 0) + (16 if mode == "dma" else 1)
        return self.cnt[inc]

    def now(self, sem):
        return self.cnt.get(sem, 0)


def build_program(dbg=()):
    nc = bass.Bass("TRN2", target_bir_lowering=False)
    kb = KB()

    xo = nc.dram_tensor("xo", [R, D], F32, kind="ExternalInput").ap()
    xp = nc.dram_tensor("xp", [R, D], F32, kind="ExternalInput").ap()
    wq_d = nc.dram_tensor("wq", [D, D], F32, kind="ExternalInput").ap()
    wk_d = nc.dram_tensor("wk", [D, D], F32, kind="ExternalInput").ap()
    wv_d = nc.dram_tensor("wv", [D, D], F32, kind="ExternalInput").ap()
    wo_d = nc.dram_tensor("wo", [D, D], F32, kind="ExternalInput").ap()
    wg_d = nc.dram_tensor("wg", [D, I2], F32, kind="ExternalInput").ap()
    wu_d = nc.dram_tensor("wu", [D, I2], F32, kind="ExternalInput").ap()
    wd_d = nc.dram_tensor("wd", [I2, D], F32, kind="ExternalInput").ap()
    fg_d = nc.dram_tensor("fg", [I3, D], F32, kind="ExternalInput").ap()
    fu_d = nc.dram_tensor("fu", [I3, D], F32, kind="ExternalInput").ap()
    fd_d = nc.dram_tensor("fd", [D, I3], F32, kind="ExternalInput").ap()
    anw_d = nc.dram_tensor("anw_b", [128, D], F32, kind="ExternalInput").ap()
    mnw_d = nc.dram_tensor("mnw_b", [128, D], F32, kind="ExternalInput").ap()
    onw_d = nc.dram_tensor("onw_b", [128, 128], F32, kind="ExternalInput").ap()
    tri_d = nc.dram_tensor("triu", [128, 128], F32, kind="ExternalInput").ap()
    idn_d = nc.dram_tensor("ident", [128, 128], BF16, kind="ExternalInput").ap()
    pmask_d = nc.dram_tensor("pmask", [128, 1], F32, kind="ExternalInput").ap()
    out_d = nc.dram_tensor("out", [R, D], F32, kind="ExternalOutput").ap()
    xns_d = nc.dram_tensor("xn_scr", [R, D], F32).ap()
    ss_d = nc.dram_tensor("s_scr", [R, D], F32).ap()
    xqs_d = nc.dram_tensor("xq_scr", [R, D], BF16).ap()
    fgq_d = nc.dram_tensor("fgq_scr", [I3, D], BF16).ap()
    fuq_d = nc.dram_tensor("fuq_scr", [I3, D], BF16).ap()
    fdq_d = nc.dram_tensor("fdq_scr", [D, I3], BF16).ap()
    dbg_outs = {}
    if "WQ" in dbg:
        dbg_outs["WQ"] = nc.dram_tensor("dbg_WQ", [I3, D], BF16, kind="ExternalOutput").ap()
        dbg_outs["WD"] = nc.dram_tensor("dbg_WD", [D, I3], BF16, kind="ExternalOutput").ap()
        dbg_outs["XQ"] = nc.dram_tensor("dbg_XQ", [R, D], BF16, kind="ExternalOutput").ap()
        dbg_outs["H0"] = nc.dram_tensor("dbg_H0", [128, 512], F32, kind="ExternalOutput").ap()
        dbg_outs["HQ0"] = nc.dram_tensor("dbg_HQ0", [128, 512], BF16, kind="ExternalOutput").ap()
        dbg_outs["G0"] = nc.dram_tensor("dbg_G0", [128, D], F32, kind="ExternalOutput").ap()
        dbg_outs["SL0"] = nc.dram_tensor("dbg_SL0", [128, 4096], BF16, kind="ExternalOutput").ap()
        dbg_outs["SL1"] = nc.dram_tensor("dbg_SL1", [128, 4096], BF16, kind="ExternalOutput").ap()
        dbg_outs["FU"] = nc.dram_tensor("dbg_FU", [I3, D], BF16, kind="ExternalOutput").ap()
        dbg_outs["WS3"] = nc.dram_tensor("dbg_WS3", [128, D], BF16, kind="ExternalOutput").ap()
        dbg_outs["GP0"] = nc.dram_tensor("dbg_GP0", [128, 512], F32, kind="ExternalOutput").ap()
        dbg_outs["UP0"] = nc.dram_tensor("dbg_UP0", [128, 512], F32, kind="ExternalOutput").ap()
    for name in [n for n in dbg if n != "WQ"]:
        shp = {"xn": [R, D], "x2": [R, D], "s": [R, D], "g": [R, D], "S": [128, H * 128],
               "PFX": [128, 4 * D], "MIX": [128, 6 * D], "P0": [128, 4 * D], "XP0": [128, 2 * D]}.get(name)
        dbg_outs[name] = nc.dram_tensor("dbg_" + name, shp, F32, kind="ExternalOutput").ap()

    st_ctx = ExitStack()
    en = st_ctx.enter_context

    XN = en(nc.sbuf_tensor("XN", [128, NCH * D], F32))
    H2F = en(nc.sbuf_tensor("H2F", [128, NCH * D], BF16))
    ONF = H2F[:, 0:D]
    QF = H2F[:, D:2 * D]
    KF = H2F[:, 2 * D:3 * D]
    H1F = H2F[:, 3 * D:4 * D]
    WA = en(nc.sbuf_tensor("WA", [128, 4 * NK * D], BF16))
    XP = en(nc.sbuf_tensor("XP", [128, 2 * D], F32))
    H1T = en(nc.sbuf_tensor("H1T", [128, D], BF16))
    QT = en(nc.sbuf_tensor("QT", [128, D], BF16))
    KT = en(nc.sbuf_tensor("KT", [128, D], BF16))
    VT = en(nc.sbuf_tensor("VT", [128, D], BF16))
    ATN = en(nc.sbuf_tensor("ATN", [128, 2 * 128], BF16))
    O32 = en(nc.sbuf_tensor("O32", [128, D], F32))
    S32 = en(nc.sbuf_tensor("S32", [128, H * 128], F32))
    SBF = en(nc.sbuf_tensor("SBF", [128, H * 128], BF16))
    WB1 = en(nc.sbuf_tensor("WB1", [128, D], F32))
    ONW = en(nc.sbuf_tensor("ONW", [128, 128], F32))
    TRI = en(nc.sbuf_tensor("TRI", [128, 128], F32))
    IDN = en(nc.sbuf_tensor("IDN", [128, 128], BF16))
    PAN = en(nc.sbuf_tensor("PAN", [128, 192], F32))
    PMSK = en(nc.sbuf_tensor("PMSK", [128, 1], F32))
    QSTG = XP[:, 1024:2048]
    QZ = en(nc.sbuf_tensor("QZ", [128, 1024], F32))
    QSG = en(nc.sbuf_tensor("QSG", [128, 1024], F32))
    QCB = en(nc.sbuf_tensor("QCB", [128, 1024], F32))
    QOUT = en(nc.sbuf_tensor("QOUT", [128, 1024], BF16))
    QPAN = en(nc.sbuf_tensor("QPAN", [128, 192], F32))
    HT = O32[:, 512:1024]
    JNK = XP[:, 0:512]
    SGT = KT[:]
    HQF = VT[:]
    WSTG = QT[:]

    PB = [en(nc.psum_tensor(f"PB{i}", [128, 512], F32)) for i in range(8)]
    PBbf = [PB[i].ap().bitcast(BF16) for i in range(8)]

    sems = {}
    for s in ("LD", "LW", "T", "V", "S", "ST", "SP", "SQ", "SD",
              "LDX", "LDP0", "LDP1", "LC", "W3A", "W3B", "LQ0", "LQ1", "LW4", "LH",
              "LX0", "LX1", "LZ0", "LZ1", "SQX"):
        sems[s] = en(nc.semaphore(s))

    def sy(fn, waits=(), inc="LD", mode="dma"):
        return kb.ev("sync", fn, waits, inc, mode)

    def gp(fn, waits=(), inc="LW", mode="dma"):
        return kb.ev("gpsimd", fn, waits, inc, mode)

    def tv(fn, waits=(), inc="T", mode="inst"):
        return kb.ev("tensor", fn, waits, inc, mode)

    def vv(fn, waits=(), inc="V", mode="inst"):
        return kb.ev("vector", fn, waits, inc, mode)

    def sc(fn, waits=(), inc="S", mode="inst"):
        return kb.ev("scalar", fn, waits, inc, mode)

    def ksl(buf, k, w=128):
        return buf[:, k * w:(k + 1) * w]

    def wsrc(d):
        return d.rearrange("(k p) n -> p k n", p=128)

    sy(lambda e: e.dma_start(out=TRI[:], in_=tri_d), inc="LC")
    sy(lambda e: e.dma_start(out=IDN[:], in_=idn_d), inc="LC")
    sy(lambda e: e.dma_start(out=ONW[:], in_=onw_d), inc="LC")
    ld_const = sy(lambda e: e.dma_start(out=WB1[:], in_=anw_d), inc="LC")
    ld_pmsk = sy(lambda e: e.dma_start(out=PMSK[:], in_=pmask_d), inc="LC")
    ld0 = ld_const
    gp(lambda e: e.dma_start(out=WA[:, 0 * NK * D:1 * NK * D].rearrange("p (k n) -> p k n", k=NK), in_=wsrc(wq_d)))
    gp(lambda e: e.dma_start(out=WA[:, 1 * NK * D:2 * NK * D].rearrange("p (k n) -> p k n", k=NK), in_=wsrc(wk_d)))
    lw_v = gp(lambda e: e.dma_start(out=WA[:, 2 * NK * D:3 * NK * D].rearrange("p (k n) -> p k n", k=NK), in_=wsrc(wv_d)))
    lw_o = gp(lambda e: e.dma_start(out=WA[:, 3 * NK * D:4 * NK * D].rearrange("p (k n) -> p k n", k=NK), in_=wsrc(wo_d)))
    WQs = WA[:, 0 * NK * D:1 * NK * D]
    WKs = WA[:, 1 * NK * D:2 * NK * D]
    WVs = WA[:, 2 * NK * D:3 * NK * D]
    WOs = WA[:, 3 * NK * D:4 * NK * D]

    v_setup = vv(lambda e: [e.memset(PAN[:], 0.0), e.memset(QPAN[:], 0.0), e.memset(S32[:], 0.0)][-1], mode="drain")
    sc(lambda e: e.preload_act_table(ACT.Sqrt), inc=None)

    qst = {"ssg": 0, "st": 0, "vqf": 0, "tcons": 0, "stSQ": 0, "stSQX": 0}

    def emit_quant_units(units, to_sbuf=False, store_sem="SQ"):
        n = len(units)
        lds = [0] * n
        ldsem = [None] * n
        for ui, u in enumerate(units):
            if u.get("src") is not None:
                ldsem[ui] = "LQ%d" % u["slot"]
                lds[ui] = sy(lambda e, u=u: e.dma_start(out=QSTG[:, u["slot"] * 512:u["slot"] * 512 + u["w"]],
                                                        in_=u["src"]),
                             waits=[("S", qst["ssg"]), ("V", qst.get("ldv", 0))], inc=ldsem[ui])
        ssg = qst["ssg"]
        for ui, u in enumerate(units):
            j, w = u["slot"], u["w"]
            src_sb = QSTG[:, j * 512:j * 512 + w] if u.get("src") is not None else u["sb"]
            waits = ([(ldsem[ui], lds[ui])] if u.get("src") is not None else [("V", u.get("vwait", 0))])
            waits = waits + [("V", qst["vqf"])]
            sc(lambda e, src_sb=src_sb, j=j, w=w: e.activation(out=QZ[:, j * 512:j * 512 + w], in_=src_sb, func=ACT.Abs),
               waits=waits, inc=None)
            ssg = sc(lambda e, src_sb=src_sb, j=j, w=w: e.sign(out=QSG[:, j * 512:j * 512 + w], in_=src_sb),
                     mode="drain")
        qst["ssg"] = ssg
        for ui, u in enumerate(units):
            j, w = u["slot"], u["w"]
            vv(lambda e, j=j, w=w: e.tensor_reduce(
                out=QPAN[:, j * 32:j * 32 + w // 16],
                in_=QZ[:, j * 512:j * 512 + w].rearrange("p (b k) -> p b k", k=16),
                axis=AX.X, op=ALU.max, apply_absolute_value=True),
               waits=[("S", ssg)], mode="drain" if ui == n - 1 else "inst")
        for ui, u in enumerate(units):
            j, w = u["slot"], u["w"]
            vv(lambda e, j=j, w=w: e.tensor_scalar(
                out=QPAN[:, 64 + j * 32:64 + j * 32 + w // 16], in0=QPAN[:, j * 32:j * 32 + w // 16],
                scalar1=1.0 / 12, scalar2=5e-13, op0=ALU.mult, op1=ALU.max),
               mode="drain" if ui == n - 1 else "inst")
        for ui, u in enumerate(units):
            j, w = u["slot"], u["w"]
            vv(lambda e, j=j, w=w: e.reciprocal(
                out=QPAN[:, 128 + j * 32:128 + j * 32 + w // 16], in_=QPAN[:, 64 + j * 32:64 + j * 32 + w // 16]),
               mode="drain" if ui == n - 1 else "inst")
        for u in units:
            j, w = u["slot"], u["w"]
            invb = QPAN[:, 128 + j * 32:128 + j * 32 + w // 16].unsqueeze(2).broadcast_to([128, w // 16, 16])
            vv(lambda e, j=j, w=w, invb=invb: e.scalar_tensor_tensor(
                out=QZ[:, j * 512:j * 512 + w].rearrange("p (b k) -> p b k", k=16),
                in0=QZ[:, j * 512:j * 512 + w].rearrange("p (b k) -> p b k", k=16),
                scalar=0.5, in1=invb, op0=ALU.mult, op1=ALU.mult))

        def chain(m, first=False):
            for u in units:
                j, w = u["slot"], u["w"]
                zj = QZ[:, j * 512:j * 512 + w]
                cj = QCB[:, j * 512:j * 512 + w]
                if first:
                    vv(lambda e, zj=zj, cj=cj, m=m: e.tensor_scalar(out=cj, in0=zj, scalar1=m, scalar2=None,
                                                                    op0=ALU.is_gt))
                else:
                    vv(lambda e, zj=zj, cj=cj, m=m: e.scalar_tensor_tensor(out=cj, in0=zj, scalar=m, in1=cj,
                                                                           op0=ALU.is_gt, op1=ALU.add))

        chain(2.5, True)
        chain(3.5)
        chain(5.0)
        chain(5.0)
        for u in units:
            j, w = u["slot"], u["w"]
            cj = QCB[:, j * 512:j * 512 + w]
            vv(lambda e, cj=cj: e.tensor_scalar(out=cj, in0=cj, scalar1=2.0, scalar2=None, op0=ALU.mult))
        chain(0.25)
        chain(0.75)
        chain(1.25)
        chain(1.75)
        for u in units:
            j, w = u["slot"], u["w"]
            sclb = QPAN[:, 64 + j * 32:64 + j * 32 + w // 16].unsqueeze(2).broadcast_to([128, w // 16, 16])
            vv(lambda e, j=j, w=w, sclb=sclb: e.tensor_tensor(
                out=QSG[:, j * 512:j * 512 + w].rearrange("p (b k) -> p b k", k=16),
                in0=QSG[:, j * 512:j * 512 + w].rearrange("p (b k) -> p b k", k=16), in1=sclb, op=ALU.mult))
        vqf = qst["vqf"]
        for ui, u in enumerate(units):
            j, w = u["slot"], u["w"]
            waits = ([(store_sem, qst["st" + store_sem])] if not to_sbuf else [("T", qst["tcons"])])
            vqf = vv(lambda e, j=j, w=w: e.tensor_tensor(
                out=QOUT[:, j * 512:j * 512 + w], in0=QCB[:, j * 512:j * 512 + w],
                in1=QSG[:, j * 512:j * 512 + w], op=ALU.mult),
                waits=waits if ui == 0 else [], mode="drain" if ui == n - 1 else "inst")
        qst["vqf"] = vqf
        if not to_sbuf:
            stq = qst["st" + store_sem]
            for u in units:
                j, w = u["slot"], u["w"]
                stq = sy(lambda e, u=u, j=j, w=w: e.dma_start(out=u["dst"], in_=QOUT[:, j * 512:j * 512 + w]),
                         waits=[("V", vqf)], inc=store_sem)
            qst["st" + store_sem] = stq
        return vqf

    ld_xo_all = sy(lambda e: e.dma_start(out=XN[:].rearrange("p (c n) -> p c n", c=NCH),
                                         in_=xo.rearrange("(c p) n -> p c n", p=128)), inc="LDX")
    s_sq = 0
    for i in range(NCH):
        vv(lambda e, i=i: e.reduce_sum(out=PAN[:, i:i + 1], in_=ksl(XN, i, D), axis=AX.X),
           waits=[("LDX", ld_xo_all), ("V", v_setup)])
        s_sq = sc(lambda e, i=i: e.activation(out=O32[:], in_=ksl(XN, i, D), func=ACT.Square,
                                              accum_out=PAN[:, 16 + i:17 + i]),
                  waits=[("LDX", ld_xo_all), ("V", v_setup)], mode="drain")
    vv(lambda e: e.tensor_scalar(out=PAN[:, 32:48], in0=PAN[:, 0:16], scalar1=1.0 / D, scalar2=None, op0=ALU.mult),
       waits=[("S", s_sq)], mode="drain")
    vv(lambda e: e.tensor_scalar(out=PAN[:, 144:160], in0=PAN[:, 16:32], scalar1=1.0 / D, scalar2=EPS,
                                 op0=ALU.mult, op1=ALU.add), mode="drain")
    vv(lambda e: e.tensor_tensor(out=PAN[:, 160:176], in0=PAN[:, 32:48], in1=PAN[:, 32:48], op=ALU.mult), mode="drain")
    vv(lambda e: e.tensor_tensor(out=PAN[:, 144:160], in0=PAN[:, 144:160], in1=PAN[:, 160:176], op=ALU.subtract),
       mode="drain")
    v_tr = vv(lambda e: e.reciprocal(out=PAN[:, 160:176], in_=PAN[:, 144:160]), mode="drain")
    s_tr = sc(lambda e: e.sqrt(out=PAN[:, 48:64], in_=PAN[:, 160:176]), waits=[("V", v_tr)], mode="drain")
    v_ln = [0] * NCH
    s_rms = 0
    for i in range(NCH):
        v_ln[i] = vv(lambda e, i=i: e.tensor_scalar(out=ksl(XN, i, D), in0=ksl(XN, i, D),
                                                    scalar1=PAN[:, 32 + i:33 + i], scalar2=PAN[:, 48 + i:49 + i],
                                                    op0=ALU.subtract, op1=ALU.mult),
                     waits=[("S", s_tr)], mode="drain")
        s_rms = sc(lambda e, i=i: e.activation(out=O32[:], in_=ksl(XN, i, D), func=ACT.Square,
                                               accum_out=PAN[:, 64 + i:65 + i]),
                   waits=[("V", v_ln[i])], mode="drain")
    st_xn_spill = sy(lambda e: e.dma_start(out=xns_d.rearrange("(c p) n -> p c n", p=128),
                                           in_=XN[:].rearrange("p (c n) -> p c n", c=NCH)),
                     waits=[("V", v_ln[NCH - 1])], inc="SP")
    if "xn" in dbg:
        sy(lambda e: e.dma_start(out=dbg_outs["xn"].rearrange("(c p) n -> p c n", p=128),
                                 in_=XN[:].rearrange("p (c n) -> p c n", c=NCH)),
           waits=[("V", v_ln[NCH - 1])], inc="SP")
        st_xn_spill = kb.now("SP")
    sd_xn = kb.now("SD")
    vv(lambda e: e.tensor_scalar(out=PAN[:, 144:160], in0=PAN[:, 64:80], scalar1=1.0 / D, scalar2=EPS,
                                 op0=ALU.mult, op1=ALU.add),
       waits=[("S", s_rms)], mode="drain")
    v_tr = vv(lambda e: e.reciprocal(out=PAN[:, 160:176], in_=PAN[:, 144:160]), mode="drain")
    s_rstd1 = sc(lambda e: e.sqrt(out=PAN[:, 80:96], in_=PAN[:, 160:176]), waits=[("V", v_tr)], mode="drain")

    xq_ssg = [0] * NCH

    def emit_xq(i):
        emit_quant_units([
            dict(sb=ksl(XN, i, D)[:, 0:512], vwait=v_ln[i],
                 dst=xqs_d[i * 128:(i + 1) * 128, 0:512], w=512, slot=0),
            dict(sb=ksl(XN, i, D)[:, 512:1024], vwait=v_ln[i],
                 dst=xqs_d[i * 128:(i + 1) * 128, 512:1024], w=512, slot=1),
        ], store_sem="SQX")
        xq_ssg[i] = qst["ssg"]

    wq_units = []
    for (src_d, dst_dq, rows, cols) in ((fg_d, fgq_d, I3, D), (fu_d, fuq_d, I3, D)):
        for rb in range(rows // 128):
            for cb in range(cols // 512):
                wq_units.append(dict(src=src_d[rb * 128:(rb + 1) * 128, cb * 512:(cb + 1) * 512],
                                     dst=dst_dq[rb * 128:(rb + 1) * 128, cb * 512:(cb + 1) * 512], w=512))
    for rb in range(D // 128):
        for cb in range((I3 + 511) // 512):
            w = min(512, I3 - cb * 512)
            wq_units.append(dict(src=fd_d[rb * 128:(rb + 1) * 128, cb * 512:cb * 512 + w],
                                 dst=fdq_d[rb * 128:(rb + 1) * 128, cb * 512:cb * 512 + w], w=w))
    qi = [0]

    def emit_wq(npairs):
        for _ in range(npairs):
            if qi[0] >= len(wq_units):
                return
            batch = wq_units[qi[0]:qi[0] + 2]
            qi[0] += len(batch)
            for j, u in enumerate(batch):
                u["slot"] = j
            emit_quant_units(batch)

    t_h1tr = 0
    v_free_pb0 = 0
    v_free_kv = 0
    t_sprev = 0
    ld_xp = [0] * NCH
    ld_xp[0] = sy(lambda e: e.dma_start(out=XP[:, 0:D], in_=xp[0:128, :]), inc="LDP0")
    ld_xp[1] = sy(lambda e: e.dma_start(out=XP[:, D:2 * D], in_=xp[128:256, :]), inc="LDP1")
    for i in range(NCH):
        pg = i % 2
        xpc = XP[:, pg * D:(pg + 1) * D]
        if "XP0" in dbg and i == 0:
            xp0_gate = sy(lambda e: e.dma_start(out=dbg_outs["XP0"][:, 0:D], in_=XP[:, 0:D]),
                          waits=[("LDP0", ld_xp[0])], inc="SD")
            kb.ev("vector", None, waits=[("SD", xp0_gate)], inc=None)
        v1 = vv(lambda e, xpc=xpc: e.reduce_sum(out=PAN[:, 176:177], in_=xpc, axis=AX.X),
                waits=[("LDP%d" % (i % 2), ld_xp[i])], mode="drain")
        s1 = sc(lambda e, xpc=xpc: e.activation(out=O32[:], in_=xpc, func=ACT.Square,
                                                accum_out=PAN[:, 177:178]),
                waits=[("LDP%d" % (i % 2), ld_xp[i]), ("V", v_setup)], mode="drain")
        vv(lambda e: e.tensor_scalar(out=PAN[:, 178:180], in0=PAN[:, 176:178], scalar1=1.0 / D, scalar2=None,
                                     op0=ALU.mult), waits=[("S", s1), ("V", v1)], mode="drain")
        vv(lambda e: e.tensor_tensor(out=PAN[:, 180:181], in0=PAN[:, 178:179], in1=PAN[:, 178:179], op=ALU.mult),
           mode="drain")
        vv(lambda e: e.tensor_tensor(out=PAN[:, 181:182], in0=PAN[:, 179:180], in1=PAN[:, 180:181], op=ALU.subtract),
           mode="drain")
        vv(lambda e: e.tensor_scalar(out=PAN[:, 181:182], in0=PAN[:, 181:182], scalar1=EPS, scalar2=None, op0=ALU.add),
           mode="drain")
        v1 = vv(lambda e: e.reciprocal(out=PAN[:, 182:183], in_=PAN[:, 181:182]), mode="drain")
        s1 = sc(lambda e: e.sqrt(out=PAN[:, 183:184], in_=PAN[:, 182:183]), waits=[("V", v1)], mode="drain")
        v1 = vv(lambda e, xpc=xpc: e.tensor_scalar(out=xpc, in0=xpc, scalar1=PAN[:, 178:179], scalar2=PAN[:, 183:184],
                                                   op0=ALU.subtract, op1=ALU.mult),
                waits=[("S", s1)], mode="drain")
        s1 = sc(lambda e, xpc=xpc: e.activation(out=O32[:], in_=xpc, func=ACT.Square,
                                                accum_out=PAN[:, 185:186]), waits=[("V", v1)], mode="drain")
        v1 = vv(lambda e: e.tensor_scalar(out=PAN[:, 177:178], in0=PAN[:, 185:186], scalar1=1.0 / D, scalar2=EPS,
                                          op0=ALU.mult, op1=ALU.add), waits=[("S", s1)], mode="drain")
        v1 = vv(lambda e: e.reciprocal(out=PAN[:, 178:179], in_=PAN[:, 177:178]), mode="drain")
        s1 = sc(lambda e: e.sqrt(out=PAN[:, 179:180], in_=PAN[:, 178:179]), waits=[("V", v1)], mode="drain")
        v_h1 = vv(lambda e, xpc=xpc: e.scalar_tensor_tensor(out=H1T[:], in0=xpc, scalar=PAN[:, 179:180], in1=WB1[:],
                                                            op0=ALU.mult, op1=ALU.mult),
                  waits=[("S", s1), ("LC", ld_const), ("T", t_h1tr)], mode="drain")
        if i + 2 < NCH:
            ld_xp[i + 2] = sy(lambda e, i=i, pg=pg: e.dma_start(out=XP[:, pg * D:(pg + 1) * D],
                                                                in_=xp[(i + 2) * 128:(i + 3) * 128, :]),
                              waits=[("V", v_h1)], inc="LDP%d" % pg)
        t1 = 0
        for k in range(NK):
            t1 = tv(lambda e, k=k: e.transpose(ksl(PBbf[0], k), ksl(H1T, k), IDN[:]),
                    waits=([("V", v_h1), ("V", v_free_pb0), ("LC", ld0), ("LW", lw_o)] if k == 0 else []))
        t_h1tr = t1
        v_h1f = vv(lambda e: e.tensor_copy(out=H1F[:], in_=PBbf[0][:]), waits=[("T", t1)], mode="inst")
        v_free_pb0 = v_h1f
        tk_ = 0
        for half in range(2):
            for k in range(NK):
                tk_ = tv(lambda e, half=half, k=k: e.matmul(PB[1 + half][:], ksl(H1F, k),
                                                            WKs[:, k * D + half * 512:k * D + half * 512 + 512],
                                                            start=(k == 0), stop=(k == NK - 1)),
                         waits=([("V", v_h1f), ("V", v_free_kv)] if (half == 0 and k == 0) else []))
        for half in range(2):
            for k in range(NK):
                tk_ = tv(lambda e, half=half, k=k: e.matmul(PB[3 + half][:], ksl(H1F, k),
                                                            WVs[:, k * D + half * 512:k * D + half * 512 + 512],
                                                            start=(k == 0), stop=(k == NK - 1)))
        v_kt = vv(lambda e: [e.tensor_copy(out=KT[:, 0:512], in_=PB[1][:]),
                             e.tensor_copy(out=KT[:, 512:1024], in_=PB[2][:]),
                             e.tensor_copy(out=VT[:, 0:512], in_=PB[3][:]),
                             e.tensor_copy(out=VT[:, 512:1024], in_=PB[4][:])][-1],
                  waits=[("T", tk_)], mode="inst")
        v_free_kv = v_kt
        for h in range(H):
            bank, off = (6 + h // 4), (h % 4) * 128
            t_sprev = tv(lambda e, h=h, bank=bank, off=off: e.matmul(
                PB[bank][:, off:off + 128], ksl(KT, h), ksl(VT, h),
                start=(h % 4 == 0), stop=(h % 4 == 3), skip_group_check=True),
                waits=([("V", v_kt)] + ([("V", v_supd_p)] if i > 0 else [("V", v_setup)])) if h == 0 else [])
        v_supd_p = vv(lambda e: [e.tensor_tensor(out=S32[:, 0:512], in0=S32[:, 0:512], in1=PB[6][:], op=ALU.add),
                                 e.tensor_tensor(out=S32[:, 512:1024], in0=S32[:, 512:1024], in1=PB[7][:],
                                                 op=ALU.add)][-1],
                      waits=[("T", t_sprev)], mode="inst")
        if "P0" in dbg and i == 0:
            sy(lambda e: e.dma_start(out=dbg_outs["P0"][:, 0:D].bitcast(BF16)[:, 0:D], in_=KT[:]),
               waits=[("V", v_supd_p)], inc="SD")
            sy(lambda e: e.dma_start(out=dbg_outs["P0"][:, D:2 * D].bitcast(BF16)[:, 0:D], in_=VT[:]), inc="SD")
            sy(lambda e: e.dma_start(out=dbg_outs["P0"][:, 2 * D:3 * D], in_=S32[:]), inc="SD")
            p0_gate = sy(lambda e: e.dma_start(out=dbg_outs["P0"][:, 3 * D:4 * D].bitcast(BF16)[:, 0:D], in_=H1F[:]),
                         inc="SD")
            kb.ev("vector", None, waits=[("SD", p0_gate)], inc=None)
        if "PFX" in dbg and i == NCH - 1:
            sy(lambda e, xpc=xpc: e.dma_start(out=dbg_outs["PFX"][:, 0:D], in_=xpc),
               waits=[("V", v_kt)], inc="SD")
            sy(lambda e: e.dma_start(out=dbg_outs["PFX"][:, D:2 * D].bitcast(BF16)[:, 0:D], in_=H1T[:]), inc="SD")
            sy(lambda e: e.dma_start(out=dbg_outs["PFX"][:, 2 * D:3 * D].bitcast(BF16)[:, 0:D], in_=KT[:]), inc="SD")
            pfx_gate = sy(lambda e: e.dma_start(out=dbg_outs["PFX"][:, 3 * D:4 * D].bitcast(BF16)[:, 0:D], in_=VT[:]), inc="SD")
            kb.ev("vector", None, waits=[("SD", pfx_gate)], inc=None)
        emit_xq(i)
    v_msk = vv(lambda e: e.tensor_scalar(out=S32[:], in0=S32[:], scalar1=PMSK[:], scalar2=None, op0=ALU.mult),
               waits=[("V", v_supd_p), ("LC", ld_pmsk)], mode="drain")
    v_sbf = vv(lambda e: e.tensor_copy(out=SBF[:], in_=S32[:]), mode="inst")
    dbg_s_gate = 0
    if "S" in dbg:
        dbg_s_gate = sy(lambda e: e.dma_start(out=dbg_outs["S"], in_=S32[:]), waits=[("V", v_sbf)], inc="SD")

    qst["ldv"] = kb.now("V")
    v_x2 = [0] * NCH
    v_oev_prev = 0
    s_h2_prev = 0
    s_or_prev = 0
    v_supd_prev = 0
    s_h2_last = 0
    for i in range(NCH):
        v_h1 = vv(lambda e, i=i: e.scalar_tensor_tensor(out=H1T[:], in0=ksl(XN, i, D), scalar=PAN[:, 80 + i:81 + i],
                                                        in1=WB1[:], op0=ALU.mult, op1=ALU.mult),
                  waits=[("S", s_rstd1), ("V", v_sbf), ("T", t_h1tr), ("S", xq_ssg[i])], mode="drain")
        t1 = 0
        for k in range(NK):
            t1 = tv(lambda e, k=k: e.transpose(ksl(PBbf[0], k), ksl(H1T, k), IDN[:]),
                    waits=([("V", v_h1), ("V", v_free_pb0)] if k == 0 else []))
        t_h1tr = t1
        v_h1f = vv(lambda e: e.tensor_copy(out=H1F[:], in_=PBbf[0][:]), waits=[("T", t1)], mode="inst")
        v_free_pb0 = v_h1f

        def proj(Wsb, dst, v_ready):
            tlast = 0
            for half in range(2):
                for k in range(NK):
                    tlast = tv(lambda e, half=half, k=k, Wsb=Wsb: e.matmul(
                        PB[1 + half][:], ksl(H1F, k), Wsb[:, k * D + half * 512:k * D + half * 512 + 512],
                        start=(k == 0), stop=(k == NK - 1)),
                        waits=([("V", v_ready)] if (half == 0 and k == 0) else []))
            return tlast, vv(lambda e, dst=dst: [e.tensor_copy(out=dst[:, 0:512], in_=PB[1][:]),
                                                 e.tensor_copy(out=dst[:, 512:1024], in_=PB[2][:])][-1],
                             waits=[("T", tlast)], mode="inst")

        _, v_q = proj(WQs, QT, v_h1f)
        _, v_k = proj(WKs, KT, v_q)
        _, v_v = proj(WVs, VT, v_k)
        t1 = 0
        for k in range(NK):
            t1 = tv(lambda e, k=k: e.transpose(ksl(PBbf[0], k), ksl(QT, k), IDN[:]),
                    waits=([("V", v_q), ("V", v_free_pb0)] if k == 0 else []))
        v_qf = vv(lambda e: e.tensor_copy(out=QF[:], in_=PBbf[0][:]), waits=[("T", t1)], mode="inst")
        t1 = 0
        for k in range(NK):
            t1 = tv(lambda e, k=k: e.transpose(ksl(PBbf[0], k), ksl(KT, k), IDN[:]),
                    waits=([("V", v_k), ("V", v_qf)] if k == 0 else []))
        v_kf = vv(lambda e: e.tensor_copy(out=KF[:], in_=PBbf[0][:]), waits=[("T", t1)], mode="inst")
        v_free_pb0 = v_kf
        v_atn = {}
        t_o = [0] * H
        t_su = [0] * H
        for h in range(H):
            abank = 3 if h % 2 == 0 else 0
            t_at = tv(lambda e, h=h, abank=abank: e.matmul(PB[abank][:, 0:128], ksl(KF, h), ksl(QF, h),
                                                           start=True, stop=True),
                      waits=([("V", v_kf), ("V", v_v)] if h == 0 else
                             ([("V", v_kf)] if h == 1 else [("V", v_atn[h - 2])])))
            v_atn[h] = vv(lambda e, h=h, abank=abank: e.tensor_tensor(out=ATN[:, (h % 2) * 128:(h % 2) * 128 + 128],
                                                                      in0=PB[abank][:, 0:128], in1=TRI[:], op=ALU.mult),
                          waits=[("T", t_at)], mode="drain")
            bank, off = 4 + h // 4, (h % 4) * 128
            t_o[h] = tv(lambda e, h=h, bank=bank, off=off: [
                e.matmul(PB[bank][:, off:off + 128], ATN[:, (h % 2) * 128:(h % 2) * 128 + 128], ksl(VT, h),
                         start=(h % 4 == 0), stop=False, skip_group_check=True),
                e.matmul(PB[bank][:, off:off + 128], ksl(QF, h), ksl(SBF, h), start=False,
                         stop=(h % 4 == 3), skip_group_check=True)][-1],
                waits=[("V", v_atn[h])] + ([("V", v_oev_prev)] if h == 0 else []))
            bank2, off2 = 6 + h // 4, (h % 4) * 128
            t_su[h] = tv(lambda e, h=h, bank2=bank2, off2=off2: e.matmul(
                PB[bank2][:, off2:off2 + 128], ksl(KT, h), ksl(VT, h),
                start=(h % 4 == 0), stop=(h % 4 == 3), skip_group_check=True),
                waits=(([("V", v_sbf if i == 0 else v_supd_prev)] +
                        ([("SD", dbg_s_gate)] if i == 0 else [])) if h == 0 else []))
        v_oev = vv(lambda e: [e.tensor_copy(out=O32[:, 0:512], in_=PB[4][:]),
                              e.tensor_copy(out=O32[:, 512:1024], in_=PB[5][:])][-1],
                   waits=[("T", t_o[H - 1]), ("S", s_h2_prev)], mode="inst")
        v_oev_prev = v_oev
        s_oss = 0
        for h in range(H):
            s_oss = sc(lambda e, h=h: e.activation(out=JNK[:, 0:128], in_=ksl(O32, h), func=ACT.Square,
                                                   accum_out=PAN[:, 128 + h:129 + h]),
                       waits=([("V", v_oev)] if h == 0 else []), mode="drain" if h == H - 1 else "inst")
        vv(lambda e: e.tensor_scalar(out=PAN[:, 144:152], in0=PAN[:, 128:136], scalar1=1.0 / 128, scalar2=EPS,
                                     op0=ALU.mult, op1=ALU.add), waits=[("S", s_oss)], mode="drain")
        v1 = vv(lambda e: e.reciprocal(out=PAN[:, 152:160], in_=PAN[:, 144:152]), mode="drain")
        s_or = sc(lambda e: e.sqrt(out=PAN[:, 136:144], in_=PAN[:, 152:160]), waits=[("V", v1)], mode="drain")
        s_or_prev = s_or
        v_on = 0
        for h in range(H):
            v_on = vv(lambda e, h=h: e.scalar_tensor_tensor(out=ksl(ONF, h), in0=ksl(O32, h),
                                                            scalar=PAN[:, 136 + h:137 + h], in1=ONW[:],
                                                            op0=ALU.mult, op1=ALU.mult),
                      waits=([("S", s_or)] if h == 0 else []), mode="drain" if h == H - 1 else "inst")
        t1 = 0
        for h in range(H):
            t1 = tv(lambda e, h=h: e.transpose(ksl(PBbf[0], h), ksl(ONF, h), IDN[:]),
                    waits=([("V", v_on), ("V", v_atn.get(7, 0))] if h == 0 else []))
        v_onf = vv(lambda e: e.tensor_copy(out=H1F[:], in_=PBbf[0][:]), waits=[("T", t1)], mode="inst")
        v_free_pb0 = v_onf
        t_wo = 0
        for half in range(2):
            for h in range(H):
                t_wo = tv(lambda e, half=half, h=h: e.matmul(
                    PB[1 + half][:], ksl(H1F, h), WOs[:, h * D + half * 512:h * D + half * 512 + 512],
                    start=(h == 0), stop=(h == H - 1)),
                    waits=([("V", v_onf), ("LW", lw_o)] if (half == 0 and h == 0) else []))
        v_x2[i] = vv(lambda e, i=i: [e.tensor_tensor(out=ksl(XN, i, D)[:, 0:512], in0=ksl(XN, i, D)[:, 0:512],
                                                     in1=PB[1][:], op=ALU.add),
                                     e.tensor_tensor(out=ksl(XN, i, D)[:, 512:1024], in0=ksl(XN, i, D)[:, 512:1024],
                                                     in1=PB[2][:], op=ALU.add)][-1],
                     waits=[("T", t_wo)] + ([("SP", st_xn_spill)] if i == 0 else []), mode="inst")
        vv(lambda e: [e.tensor_tensor(out=S32[:, 0:512], in0=S32[:, 0:512], in1=PB[6][:], op=ALU.add),
                      e.tensor_tensor(out=S32[:, 512:1024], in0=S32[:, 512:1024], in1=PB[7][:], op=ALU.add)][-1],
           waits=[("T", t_su[H - 1])])
        v_supd_prev = vv(lambda e: e.tensor_copy(out=SBF[:], in_=S32[:]), mode="inst")
        s_h2_last = sc(lambda e, i=i: e.activation(out=O32[:], in_=ksl(XN, i, D), func=ACT.Square,
                                                   accum_out=PAN[:, 96 + i:97 + i]),
                       waits=[("V", v_x2[i])], mode="drain")
        s_h2_prev = s_h2_last
        if "MIX" in dbg and i == 0:
            sy(lambda e: e.dma_start(out=dbg_outs["MIX"][:, 0:D].bitcast(BF16)[:, 0:D], in_=QT[:]),
               waits=[("V", v_on)], inc="SD")
            sy(lambda e: e.dma_start(out=dbg_outs["MIX"][:, D:2 * D].bitcast(BF16)[:, 0:D], in_=KT[:]), inc="SD")
            sy(lambda e: e.dma_start(out=dbg_outs["MIX"][:, 2 * D:3 * D].bitcast(BF16)[:, 0:D], in_=VT[:]), inc="SD")
            sy(lambda e: e.dma_start(out=dbg_outs["MIX"][:, 3 * D:4 * D], in_=O32[:]), inc="SD")
            sy(lambda e: e.dma_start(out=dbg_outs["MIX"][:, 4 * D:5 * D].bitcast(BF16)[:, 0:D], in_=ONF[:]), inc="SD")
            dbg_gate = sy(lambda e: e.dma_start(out=dbg_outs["MIX"][:, 5 * D:5 * D + 192], in_=PAN[:]), inc="SD")
            kb.ev("vector", None, waits=[("SD", dbg_gate)], inc=None)
        emit_wq(1)

    dbg_x2_gate = 0
    if "x2" in dbg:
        dbg_x2_gate = sy(lambda e: e.dma_start(out=dbg_outs["x2"].rearrange("(c p) n -> p c n", p=128),
                                               in_=XN[:].rearrange("p (c n) -> p c n", c=NCH)),
                         waits=[("V", v_x2[NCH - 1])], inc="SD")

    t_after_s2_main = kb.now("T")
    ld_mnw = sy(lambda e: e.dma_start(out=WB1[:], in_=mnw_d), waits=[("V", kb.now("V"))], inc="LC")
    vv(lambda e: e.tensor_scalar(out=PAN[:, 144:160], in0=PAN[:, 96:112], scalar1=1.0 / D, scalar2=EPS,
                                 op0=ALU.mult, op1=ALU.add), waits=[("S", s_h2_last)], mode="drain")
    v1 = vv(lambda e: e.reciprocal(out=PAN[:, 160:176], in_=PAN[:, 144:160]), mode="drain")
    s_rstd2 = sc(lambda e: e.sqrt(out=PAN[:, 112:128], in_=PAN[:, 160:176]), waits=[("V", v1)], mode="drain")
    v_h2f = [0] * NCH
    for i in range(NCH):
        v_h2 = vv(lambda e, i=i: e.scalar_tensor_tensor(out=H1T[:], in0=ksl(XN, i, D), scalar=PAN[:, 112 + i:113 + i],
                                                        in1=WB1[:], op0=ALU.mult, op1=ALU.mult),
                  waits=[("S", s_rstd2), ("LC", ld_mnw), ("T", t_h1tr)], mode="drain")
        t1 = 0
        for k in range(NK):
            t1 = tv(lambda e, k=k: e.transpose(ksl(PBbf[0], k), ksl(H1T, k), IDN[:]),
                    waits=([("V", v_h2), ("V", v_free_pb0)] if k == 0 else []))
        t_h1tr = t1
        v_h2f[i] = vv(lambda e, i=i: e.tensor_copy(out=ksl(H2F, i, D), in_=PBbf[0][:]), waits=[("T", t1)], mode="inst")
        v_free_pb0 = v_h2f[i]
        emit_wq(1)

    sc(lambda e: e.preload_act_table(ACT.Silu), waits=[("S", kb.now("S"))], inc=None)
    NT2 = I2 // 512

    def slot(s, pg):
        return WA[:, (pg * 3 + s) * 4096:(pg * 3 + s + 1) * 4096]

    lw_t = [0] * NT2
    t_after_s2 = kb.now("T")
    t_tile_done = {0: t_after_s2_main, 1: t_after_s2_main}
    v_free_d = 0
    v_s_done = 0
    v_free_gp3 = [0, 0]
    t_a2cons = [0, 0]
    pend3 = [None]
    for it in range(NT2):
        pg = it % 2
        w_wait = [("T", t_tile_done[pg])]
        wsem = "W3A" if pg == 0 else "W3B"
        gp(lambda e, it=it, pg=pg: e.dma_start(
            out=slot(0, pg).rearrange("p (k n) -> p k n", k=NK),
            in_=wsrc(wg_d)[:, :, it * 512:(it + 1) * 512]), waits=w_wait, inc=wsem)
        gp(lambda e, it=it, pg=pg: e.dma_start(
            out=slot(1, pg).rearrange("p (k n) -> p k n", k=NK),
            in_=wsrc(wu_d)[:, :, it * 512:(it + 1) * 512]), inc=wsem)
        lw_t[it] = gp(lambda e, it=it, pg=pg: e.dma_start(
            out=slot(2, pg).rearrange("p (k n) -> p k n", k=4),
            in_=wd_d.rearrange("(k p) n -> p k n", p=128)[:, it * 4:(it + 1) * 4, :]), inc=wsem)
        lw_sem = wsem

        def emit_tail3(p):
            nonlocal v_free_pb0, v_free_d, v_s_done
            (i_, v_a2_, pg_, a2b_) = p
            t1 = 0
            for k in range(4):
                t1 = tv(lambda e, k=k, a2b_=a2b_: e.transpose(ksl(PBbf[0], k),
                                                              a2b_[:, k * 128:(k + 1) * 128], IDN[:]),
                        waits=([("V", v_a2_), ("V", v_free_pb0)] if k == 0 else []))
            t_a2cons[i_ % 2] = t1
            v_a2f = vv(lambda e: e.tensor_copy(out=HQF[:, 0:512], in_=PBbf[0][:, 0:512]),
                       waits=[("T", t1)], mode="inst")
            v_free_pb0 = v_a2f
            td = 0
            for half in range(2):
                for k in range(4):
                    td = tv(lambda e, pg_=pg_, half=half, k=k: e.matmul(
                        PB[3 + half][:], ksl(HQF, k),
                        slot(2, pg_)[:, k * 1024 + half * 512:k * 1024 + half * 512 + 512],
                        start=(k == 0), stop=(k == 3)),
                        waits=([("V", v_a2f), ("V", v_free_d)] if (half == 0 and k == 0) else []))
            last = (pg_ == (NT2 - 1) % 2 and i_ == NCH - 1 and it == NT2 - 1)
            v_free_d = vv(lambda e, i_=i_: [e.tensor_tensor(out=ksl(XN, i_, D)[:, 0:512],
                                                            in0=ksl(XN, i_, D)[:, 0:512],
                                                            in1=PB[3][:], op=ALU.add),
                                            e.tensor_tensor(out=ksl(XN, i_, D)[:, 512:1024],
                                                            in0=ksl(XN, i_, D)[:, 512:1024],
                                                            in1=PB[4][:], op=ALU.add)][-1],
                          waits=[("T", td)] + ([("SD", dbg_x2_gate)] if (it == 0 and i_ == 0) else []),
                          mode="drain" if last else "inst")
            if last:
                v_s_done = v_free_d

        for i in range(NCH):
            par = i % 2
            gb, ub = (1, 2) if par == 0 else (5, 6)
            silb, a2b = (SGT[:, 0:512], SGT[:, 512:1024]) if par == 0 else (H1T[:, 0:512], H1T[:, 512:1024])
            tg = 0
            for k in range(NK):
                tg = tv(lambda e, pg=pg, i=i, k=k, gb=gb: e.matmul(
                    PB[gb][:], H2F[:, i * D + k * 128:i * D + (k + 1) * 128],
                    slot(0, pg)[:, k * 512:(k + 1) * 512], start=(k == 0), stop=(k == NK - 1)),
                    waits=([("W3A" if pg == 0 else "W3B", lw_t[it]), ("V", v_h2f[i]),
                            ("V", v_free_gp3[par])] if k == 0 else []))
            tu = 0
            for k in range(NK):
                tu = tv(lambda e, pg=pg, i=i, k=k, ub=ub: e.matmul(
                    PB[ub][:], H2F[:, i * D + k * 128:i * D + (k + 1) * 128],
                    slot(1, pg)[:, k * 512:(k + 1) * 512], start=(k == 0), stop=(k == NK - 1)))
            if pend3[0] is not None:
                emit_tail3(pend3[0])
                pend3[0] = None
            s_sg = sc(lambda e, gb=gb, silb=silb: e.activation(out=silb, in_=PB[gb][:], func=ACT.Silu),
                      waits=[("T", tu), ("V", v_free_gp3[par])], mode="drain")
            v_a2 = vv(lambda e, ub=ub, silb=silb, a2b=a2b: e.tensor_tensor(out=a2b, in0=silb, in1=PB[ub][:],
                                                                           op=ALU.mult),
                      waits=[("S", s_sg), ("T", t_a2cons[par])], mode="inst")
            v_free_gp3[par] = v_a2
            pend3[0] = (i, v_a2, pg, a2b)
            if i % 2 == 1:
                emit_wq(1)
        if pend3[0] is not None:
            emit_tail3(pend3[0])
            pend3[0] = None
        t_tile_done[pg] = kb.now("T")
    emit_wq(200)
    sy(lambda e: e.dma_start(out=ss_d.rearrange("(c p) n -> p c n", p=128),
                             in_=XN[:].rearrange("p (c n) -> p c n", c=NCH)),
       waits=[("V", v_s_done)], inc="SP")
    if "s" in dbg:
        sy(lambda e: e.dma_start(out=dbg_outs["s"].rearrange("(c p) n -> p c n", p=128),
                                 in_=XN[:].rearrange("p (c n) -> p c n", c=NCH)),
           waits=[("V", v_s_done)], inc="SD")
    st_spill_s = kb.now("SP")
    sd_spill_s = kb.now("SD")

    all_q_stores = kb.now("SQX")
    all_wq_stores = kb.now("SQ")
    if "WQ" in dbg:
        sy(lambda e: e.dma_start(out=dbg_outs["WQ"], in_=fgq_d), waits=[("SQ", all_q_stores)], inc="SD")
        sy(lambda e: e.dma_start(out=dbg_outs["WD"], in_=fdq_d), inc="SD")
        sy(lambda e: e.dma_start(out=dbg_outs["FU"], in_=fuq_d), inc="SD")
        sy(lambda e: e.dma_start(out=dbg_outs["XQ"], in_=xqs_d), inc="SD")
    v_xqf = [0] * NCH
    t_xq_tr = 0
    t_after_s3 = kb.now("T")
    for i in range(NCH):
        ldq = sy(lambda e, i=i: e.dma_start(out=H1T[:], in_=xqs_d[i * 128:(i + 1) * 128, :]),
                 waits=[("SQX", all_q_stores), ("T", t_after_s3)] if i == 0 else [("T", t_xq_tr)], inc="LH")
        t1 = 0
        for k in range(NK):
            t1 = tv(lambda e, k=k: e.transpose(ksl(PBbf[0], k), ksl(H1T, k), IDN[:]),
                    waits=([("LH", ldq), ("V", v_free_pb0), ("V", v_s_done)] if k == 0 else []))
        t_xq_tr = t1
        v_xqf[i] = vv(lambda e, i=i: e.tensor_copy(out=ksl(H2F, i, D), in_=PBbf[0][:]), waits=[("T", t1)], mode="inst")
        v_free_pb0 = v_xqf[i]

    IT3 = [(0, 512), (512, 512), (1024, 512), (1536, 512), (2048, 512), (2560, 256)]

    def slot4(s, pg):
        return WA[:, (pg * 3 + s) * 4096:(pg * 3 + s + 1) * 4096]

    t_after_s3 = kb.now("T")
    t_tile4_done = {0: t_after_s3, 1: t_after_s3}
    v_free_g = 0
    v_free_d = 0
    v_gacc = [0] * NCH
    t_wtr = t_after_s3
    for ti, (i0, itw) in enumerate(IT3):
        pg = ti % 2
        nI = (itw + 127) // 128
        v_wslot = 0
        for wslot, src_q in ((0, fgq_d), (1, fuq_d)):
            for c in range(nI):
                ldw = sy(lambda e, src_q=src_q, i0=i0, c=c: e.dma_start(
                    out=WSTG[:], in_=src_q[i0 + c * 128:i0 + (c + 1) * 128, :]),
                    waits=[("T", t_wtr), ("T", t_tile4_done[pg]), ("SQ", all_wq_stores)], inc="LW4")
                if "WQ" in dbg and ti == 0 and wslot == 1 and c == nI - 1:
                    ws3g = sy(lambda e: e.dma_start(out=dbg_outs["WS3"], in_=WSTG[:, 0:D]),
                              waits=[("LW4", ldw)], inc="SD")
                    kb.ev("tensor", None, waits=[("SD", ws3g)], inc=None)
                t1 = 0
                for k in range(NK):
                    t1 = tv(lambda e, k=k: e.transpose(ksl(PBbf[0], k), ksl(WSTG, k), IDN[:]),
                            waits=([("LW4", ldw), ("V", v_free_pb0)] if k == 0 else []))
                t_wtr = t1
                v_wslot = vv(lambda e, wslot=wslot, pg=pg, c=c, itw=itw: [
                    e.tensor_copy(out=slot4(wslot, pg)[:, k * itw + c * 128:k * itw + (c + 1) * 128],
                                  in_=ksl(PBbf[0], k)) for k in range(NK)][-1],
                    waits=[("T", t1)], mode="drain")
                v_free_pb0 = v_wslot
        for hb in range(NK):
            ldw = sy(lambda e, hb=hb, i0=i0, itw=itw: e.dma_start(
                out=WSTG[:, 0:itw], in_=fdq_d[hb * 128:(hb + 1) * 128, i0:i0 + itw]),
                waits=[("T", t_wtr), ("SQ", all_wq_stores)], inc="LW4")
            t1 = 0
            for c in range(nI):
                t1 = tv(lambda e, c=c: e.transpose(ksl(PBbf[0], c), ksl(WSTG, c), IDN[:]),
                        waits=([("LW4", ldw), ("V", v_free_pb0)] if c == 0 else []))
            t_wtr = t1
            v_wslot = vv(lambda e, pg=pg, hb=hb, nI=nI: [
                e.tensor_copy(out=slot4(2, pg)[:, c * 1024 + hb * 128:c * 1024 + (hb + 1) * 128],
                              in_=ksl(PBbf[0], c)) for c in range(nI)][-1],
                waits=[("T", t1)], mode="drain")
            v_free_pb0 = v_wslot
        if "WQ" in dbg and ti == 0:
            sl0g = sy(lambda e: e.dma_start(out=dbg_outs["SL0"], in_=slot4(0, 0)), waits=[("V", v_wslot)], inc="SD")
            sl0g = sy(lambda e: e.dma_start(out=dbg_outs["SL1"], in_=slot4(1, 0)), inc="SD")
            kb.ev("tensor", None, waits=[("SD", sl0g)], inc=None)
        pend = None
        HTp = [HT[:, 0:512], O32[:, 0:512]]
        SGTp = [SGT[:, 0:512], SGT[:, 512:1024]]

        def emit_tail(p):
            nonlocal v_free_pb0, v_free_d
            (i0_, hq_vq, nI_, pg_, itw_) = p
            for pi, i in enumerate((i0_, i0_ + 1)):
                t1 = 0
                for c in range(nI_):
                    qo = pi * 512 + c * 128
                    t1 = tv(lambda e, c=c, qo=qo: e.transpose(ksl(PBbf[0], c), QOUT[:, qo:qo + 128], IDN[:]),
                            waits=([("V", hq_vq), ("V", v_free_pb0)] if c == 0 else []))
                qst["tcons"] = t1
                v_hqf = vv(lambda e, pi=pi, itw_=itw_: e.tensor_copy(out=HQF[:, pi * 512:pi * 512 + itw_],
                                                                     in_=PBbf[0][:, 0:itw_]),
                           waits=[("T", t1)], mode="inst")
                v_free_pb0 = v_hqf
                td = 0
                for half in range(2):
                    for c in range(nI_):
                        td = tv(lambda e, pg_=pg_, pi=pi, half=half, c=c: e.matmul(
                            PB[3 + half][:], HQF[:, pi * 512 + c * 128:pi * 512 + (c + 1) * 128],
                            slot4(2, pg_)[:, c * 1024 + half * 512:c * 1024 + half * 512 + 512],
                            start=(c == 0), stop=(c == nI_ - 1)),
                            waits=([("V", v_hqf), ("V", v_free_d)] if (half == 0 and c == 0) else []))
                if ti == 0:
                    v_free_d = vv(lambda e, i=i: [e.tensor_copy(out=ksl(XN, i, D)[:, 0:512], in_=PB[3][:]),
                                                  e.tensor_copy(out=ksl(XN, i, D)[:, 512:1024], in_=PB[4][:])][-1],
                                  waits=[("T", td)] + ([("SP", st_spill_s), ("SD", sd_spill_s)] if i == 0 else []),
                                  mode="inst")
                else:
                    v_free_d = vv(lambda e, i=i: [e.tensor_tensor(out=ksl(XN, i, D)[:, 0:512],
                                                                  in0=ksl(XN, i, D)[:, 0:512], in1=PB[3][:],
                                                                  op=ALU.add),
                                                  e.tensor_tensor(out=ksl(XN, i, D)[:, 512:1024],
                                                                  in0=ksl(XN, i, D)[:, 512:1024], in1=PB[4][:],
                                                                  op=ALU.add)][-1],
                                  waits=[("T", td)], mode="inst")
                if ti == len(IT3) - 1:
                    v_gacc[i] = v_free_d

        v_free_gp = [0, 0]
        for i0 in range(0, NCH, 2):
            v_hh = [0, 0]
            for i in (i0, i0 + 1):
                par = i % 2
                gb, ub = (1, 2) if par == 0 else (5, 6)
                tg = 0
                for k in range(NK):
                    tg = tv(lambda e, pg=pg, i=i, k=k, itw=itw, gb=gb: e.matmul(
                        PB[gb][:, 0:itw], H2F[:, i * D + k * 128:i * D + (k + 1) * 128],
                        slot4(0, pg)[:, k * itw:(k + 1) * itw], start=(k == 0), stop=(k == NK - 1)),
                        waits=([("V", v_wslot), ("V", v_xqf[i]), ("V", v_free_gp[par])] if k == 0 else []))
                tu = 0
                for k in range(NK):
                    tu = tv(lambda e, pg=pg, i=i, k=k, itw=itw, ub=ub: e.matmul(
                        PB[ub][:, 0:itw], H2F[:, i * D + k * 128:i * D + (k + 1) * 128],
                        slot4(1, pg)[:, k * itw:(k + 1) * itw], start=(k == 0), stop=(k == NK - 1)))
                if i == i0 and pend is not None:
                    emit_tail(pend)
                    pend = None
                s_sg = sc(lambda e, itw=itw, gb=gb, par=par: e.activation(out=SGTp[par][:, 0:itw],
                                                                         in_=PB[gb][:, 0:itw], func=ACT.Silu),
                          waits=[("T", tu), ("V", v_free_gp[par])], mode="drain")
                v_h = vv(lambda e, itw=itw, ub=ub, par=par: e.tensor_tensor(
                    out=HTp[par][:, 0:itw], in0=SGTp[par][:, 0:itw], in1=PB[ub][:, 0:itw],
                    op=ALU.mult), waits=[("S", s_sg)], mode="inst")
                v_free_gp[par] = v_h
                v_hh[par] = v_h
            units = [dict(sb=HTp[0][:, 0:itw], vwait=v_hh[0], w=itw, slot=0),
                     dict(sb=HTp[1][:, 0:itw], vwait=v_hh[1], w=itw, slot=1)]
            hq_vq = emit_quant_units(units, to_sbuf=True)
            pend = (i0, hq_vq, nI, pg, itw)
        if pend is not None:
            emit_tail(pend)
            pend = None
        t_tile4_done[pg] = kb.now("T")

    s_g = [0] * NCH
    for i in range(NCH):
        s_g[i] = sc(lambda e, i=i: e.activation(out=ksl(XN, i, D), in_=ksl(XN, i, D), func=ACT.Silu),
                    waits=[("V", v_gacc[i])], mode="drain")
    if "g" in dbg:
        sy(lambda e: e.dma_start(out=dbg_outs["g"].rearrange("(c p) n -> p c n", p=128),
                                 in_=XN[:].rearrange("p (c n) -> p c n", c=NCH)),
           waits=[("S", s_g[NCH - 1])], inc="SD")

    WAf = WA[:].bitcast(F32)
    bufX = [WAf[:, 0:1024], WAf[:, 1024:2048]]
    bufS = [WAf[:, 2048:3072], WAf[:, 3072:4096]]
    bufT = [WAf[:, 4096:5120], WAf[:, 5120:6144]]
    t_all = kb.now("T")
    v_out = [0] * NCH
    for i in range(NCH):
        p = i % 2
        w_pre = ([("V", v_out[i - 2])] if i >= 2 else [("T", t_tile4_done[0]), ("V", qst["vqf"])])
        ldx = sy(lambda e, i=i, p=p: e.dma_start(out=bufX[p], in_=xo[i * 128:(i + 1) * 128, :]),
                 waits=w_pre, inc="LX%d" % p)
        lds = sy(lambda e, i=i, p=p: e.dma_start(out=bufS[p], in_=ss_d[i * 128:(i + 1) * 128, :]),
                 inc="LZ%d" % p)
        vv(lambda e, i=i, p=p: e.tensor_tensor(out=bufT[p], in0=bufS[p], in1=ksl(XN, i, D), op=ALU.mult),
           waits=[("LZ%d" % p, lds), ("S", s_g[i])])
        v_out[i] = vv(lambda e, i=i, p=p: e.tensor_tensor(out=bufT[p], in0=bufT[p], in1=bufX[p], op=ALU.add),
                      waits=[("LX%d" % p, ldx)], mode="drain")
        sy(lambda e, i=i, p=p: e.dma_start(out=out_d[i * 128:(i + 1) * 128, :], in_=bufT[p]),
           waits=[("V", v_out[i])], inc="ST")
    st_total = kb.now("ST")

    with (nc.Block() as block,):
        def emit_for(name):
            def body(eng):
                waited = {}
                for (waits, fn, inc, mode) in kb.evs[name]:
                    for (s, v) in waits:
                        if waited.get(s, 0) < v:
                            eng.wait_ge(sems[s], v)
                            waited[s] = v
                    inst = fn(eng) if fn is not None else None
                    if inc is not None:
                        if mode == "dma":
                            inst.then_inc(sems[inc], 16)
                        elif mode == "inst":
                            inst.then_inc(sems[inc], 1)
                        elif mode == "drain":
                            eng.drain().then_inc(sems[inc], 1)
                        elif mode == "nop":
                            eng.nop().then_inc(sems[inc], 1)
                if name == "sync":
                    eng.wait_ge(sems["ST"], st_total)
                    for s2 in ("SP", "SQ", "SD", "LDX", "LDP0", "LDP1", "LC", "LQ0", "LQ1",
                               "LW4", "LH", "LX0", "LX1", "LZ0", "LZ1", "SQX"):
                        if kb.now(s2):
                            eng.wait_ge(sems[s2], kb.now(s2))
            return body

        block.sync(emit_for("sync"))
        block.gpsimd(emit_for("gpsimd"))
        block.vector(emit_for("vector"))
        block.scalar(emit_for("scalar"))
        block.tensor(emit_for("tensor"))

    st_ctx.close()
    return nc, dbg_outs


_NC_CACHE = {}


def _get_program(dbg=()):
    key = tuple(sorted(dbg))
    if key not in _NC_CACHE:
        _NC_CACHE[key] = build_program(dbg)
    return _NC_CACHE[key]


def make_in_maps(x, wq, wk, wv, wo, anw, onw, mnw, wg, wu, wd, fgw, fuw, fdw):
    x = np.ascontiguousarray(np.asarray(x, dtype=np.float32))
    tri = np.triu(np.ones((128, 128), dtype=np.float32))
    idn = np.eye(128, dtype=ml_dtypes.bfloat16)
    anw_b = np.ascontiguousarray(np.broadcast_to(np.asarray(anw, np.float32), (128, D)))
    mnw_b = np.ascontiguousarray(np.broadcast_to(np.asarray(mnw, np.float32), (128, D)))
    onw_b = np.ascontiguousarray(np.broadcast_to(np.asarray(onw, np.float32), (128, 128)))
    shared = dict(
        wq=np.ascontiguousarray(wq, np.float32), wk=np.ascontiguousarray(wk, np.float32),
        wv=np.ascontiguousarray(wv, np.float32), wo=np.ascontiguousarray(wo, np.float32),
        wg=np.ascontiguousarray(wg, np.float32), wu=np.ascontiguousarray(wu, np.float32),
        wd=np.ascontiguousarray(wd, np.float32),
        fg=np.ascontiguousarray(fgw, np.float32), fu=np.ascontiguousarray(fuw, np.float32),
        fd=np.ascontiguousarray(fdw, np.float32),
        anw_b=anw_b, mnw_b=mnw_b, onw_b=onw_b, triu=tri, ident=idn,
    )
    zeros = np.zeros((R, D), dtype=np.float32)
    ones_m = np.ones((128, 1), dtype=np.float32)
    zeros_m = np.zeros((128, 1), dtype=np.float32)
    in_maps = []
    for c in range(8):
        b, half = c // 2, c % 2
        im = dict(shared)
        im["xo"] = np.ascontiguousarray(x[b, half * R:(half + 1) * R, :])
        im["xp"] = np.ascontiguousarray(x[b, 0:R, :]) if half == 1 else zeros
        im["pmask"] = ones_m if half == 1 else zeros_m
        in_maps.append(im)
    return in_maps


def kernel(L0_attn_hidden_states, L0_attn_attn_norm_w, L0_attn_Wq, L0_attn_Wk, L0_attn_Wv,
           L0_attn_Wo, L0_attn_attn_out_norm_w, L0_attn_mlp_norm_w, L0_attn_Wgate, L0_attn_Wup,
           L0_attn_Wdown, L0_ffn_w_gate, L0_ffn_w_up, L0_ffn_w_down):
    in_maps = make_in_maps(L0_attn_hidden_states, L0_attn_Wq, L0_attn_Wk, L0_attn_Wv, L0_attn_Wo,
                           L0_attn_attn_norm_w, L0_attn_attn_out_norm_w, L0_attn_mlp_norm_w,
                           L0_attn_Wgate, L0_attn_Wup, L0_attn_Wdown,
                           L0_ffn_w_gate, L0_ffn_w_up, L0_ffn_w_down)
    nc, _ = _get_program()
    res = run_bass_kernel_spmd(nc, in_maps, core_ids=list(range(8)))
    out = np.empty((4, 4096, D), dtype=np.float32)
    for c, r in enumerate(res.results):
        b, half = c // 2, c % 2
        out[b, half * R:(half + 1) * R, :] = r["out"]
    return out


# version 15
# speedup vs baseline: 1.0027x; 1.0027x over previous
import numpy as np
import ml_dtypes
import concourse.bass as bass
import concourse.mybir as mybir
from concourse.bass_utils import run_bass_kernel_spmd
from contextlib import ExitStack

F32 = mybir.dt.float32
BF16 = mybir.dt.bfloat16
AX = mybir.AxisListType
ALU = mybir.AluOpType
ACT = mybir.ActivationFunctionType

R = 2048
NCH = 16
D = 1024
NK = 8
H = 8
I2 = 4096
I3 = 2816
EPS = 1e-5

ENGINES = ("sync", "gpsimd", "vector", "scalar", "tensor")


class KB:
    def __init__(self):
        self.evs = {e: [] for e in ENGINES}
        self.cnt = {}

    def ev(self, eng, fn, waits=(), inc=None, mode="inst"):
        w = [(s, v) for (s, v) in waits if v and v > 0]
        self.evs[eng].append((w, fn, inc, mode))
        if inc is None:
            return None
        self.cnt[inc] = self.cnt.get(inc, 0) + (16 if mode == "dma" else 1)
        return self.cnt[inc]

    def now(self, sem):
        return self.cnt.get(sem, 0)


def build_program(dbg=()):
    nc = bass.Bass("TRN2", target_bir_lowering=False)
    kb = KB()

    xo = nc.dram_tensor("xo", [R, D], F32, kind="ExternalInput").ap()
    xp = nc.dram_tensor("xp", [R, D], F32, kind="ExternalInput").ap()
    wq_d = nc.dram_tensor("wq", [D, D], F32, kind="ExternalInput").ap()
    wk_d = nc.dram_tensor("wk", [D, D], F32, kind="ExternalInput").ap()
    wv_d = nc.dram_tensor("wv", [D, D], F32, kind="ExternalInput").ap()
    wo_d = nc.dram_tensor("wo", [D, D], F32, kind="ExternalInput").ap()
    wg_d = nc.dram_tensor("wg", [D, I2], F32, kind="ExternalInput").ap()
    wu_d = nc.dram_tensor("wu", [D, I2], F32, kind="ExternalInput").ap()
    wd_d = nc.dram_tensor("wd", [I2, D], F32, kind="ExternalInput").ap()
    fg_d = nc.dram_tensor("fg", [I3, D], F32, kind="ExternalInput").ap()
    fu_d = nc.dram_tensor("fu", [I3, D], F32, kind="ExternalInput").ap()
    fd_d = nc.dram_tensor("fd", [D, I3], F32, kind="ExternalInput").ap()
    anw_d = nc.dram_tensor("anw_b", [128, D], F32, kind="ExternalInput").ap()
    mnw_d = nc.dram_tensor("mnw_b", [128, D], F32, kind="ExternalInput").ap()
    onw_d = nc.dram_tensor("onw_b", [128, 128], F32, kind="ExternalInput").ap()
    tri_d = nc.dram_tensor("triu", [128, 128], F32, kind="ExternalInput").ap()
    idn_d = nc.dram_tensor("ident", [128, 128], BF16, kind="ExternalInput").ap()
    pmask_d = nc.dram_tensor("pmask", [128, 1], F32, kind="ExternalInput").ap()
    out_d = nc.dram_tensor("out", [R, D], F32, kind="ExternalOutput").ap()
    xns_d = nc.dram_tensor("xn_scr", [R, D], F32).ap()
    ss_d = nc.dram_tensor("s_scr", [R, D], F32).ap()
    xqs_d = nc.dram_tensor("xq_scr", [R, D], BF16).ap()
    fgq_d = nc.dram_tensor("fgq_scr", [I3, D], BF16).ap()
    fuq_d = nc.dram_tensor("fuq_scr", [I3, D], BF16).ap()
    fdq_d = nc.dram_tensor("fdq_scr", [D, I3], BF16).ap()
    dbg_outs = {}
    if "WQ" in dbg:
        dbg_outs["WQ"] = nc.dram_tensor("dbg_WQ", [I3, D], BF16, kind="ExternalOutput").ap()
        dbg_outs["WD"] = nc.dram_tensor("dbg_WD", [D, I3], BF16, kind="ExternalOutput").ap()
        dbg_outs["XQ"] = nc.dram_tensor("dbg_XQ", [R, D], BF16, kind="ExternalOutput").ap()
        dbg_outs["H0"] = nc.dram_tensor("dbg_H0", [128, 512], F32, kind="ExternalOutput").ap()
        dbg_outs["HQ0"] = nc.dram_tensor("dbg_HQ0", [128, 512], BF16, kind="ExternalOutput").ap()
        dbg_outs["G0"] = nc.dram_tensor("dbg_G0", [128, D], F32, kind="ExternalOutput").ap()
        dbg_outs["SL0"] = nc.dram_tensor("dbg_SL0", [128, 4096], BF16, kind="ExternalOutput").ap()
        dbg_outs["SL1"] = nc.dram_tensor("dbg_SL1", [128, 4096], BF16, kind="ExternalOutput").ap()
        dbg_outs["FU"] = nc.dram_tensor("dbg_FU", [I3, D], BF16, kind="ExternalOutput").ap()
        dbg_outs["WS3"] = nc.dram_tensor("dbg_WS3", [128, D], BF16, kind="ExternalOutput").ap()
        dbg_outs["GP0"] = nc.dram_tensor("dbg_GP0", [128, 512], F32, kind="ExternalOutput").ap()
        dbg_outs["UP0"] = nc.dram_tensor("dbg_UP0", [128, 512], F32, kind="ExternalOutput").ap()
    for name in [n for n in dbg if n != "WQ"]:
        shp = {"xn": [R, D], "x2": [R, D], "s": [R, D], "g": [R, D], "S": [128, H * 128],
               "PFX": [128, 4 * D], "MIX": [128, 6 * D], "P0": [128, 4 * D], "XP0": [128, 2 * D]}.get(name)
        dbg_outs[name] = nc.dram_tensor("dbg_" + name, shp, F32, kind="ExternalOutput").ap()

    st_ctx = ExitStack()
    en = st_ctx.enter_context

    XN = en(nc.sbuf_tensor("XN", [128, NCH * D], F32))
    H2F = en(nc.sbuf_tensor("H2F", [128, NCH * D], BF16))
    ONF = H2F[:, 0:D]
    QF = H2F[:, D:2 * D]
    KF = H2F[:, 2 * D:3 * D]
    H1F = H2F[:, 3 * D:4 * D]
    WA = en(nc.sbuf_tensor("WA", [128, 4 * NK * D], BF16))
    XP = en(nc.sbuf_tensor("XP", [128, 2 * D], F32))
    H1T = en(nc.sbuf_tensor("H1T", [128, D], BF16))
    QT = en(nc.sbuf_tensor("QT", [128, D], BF16))
    KT = en(nc.sbuf_tensor("KT", [128, D], BF16))
    VT = en(nc.sbuf_tensor("VT", [128, D], BF16))
    ATN = en(nc.sbuf_tensor("ATN", [128, 2 * 128], BF16))
    O32 = en(nc.sbuf_tensor("O32", [128, D], F32))
    S32 = en(nc.sbuf_tensor("S32", [128, H * 128], F32))
    SBF = en(nc.sbuf_tensor("SBF", [128, H * 128], BF16))
    WB1 = en(nc.sbuf_tensor("WB1", [128, D], F32))
    ONW = en(nc.sbuf_tensor("ONW", [128, 128], F32))
    TRI = en(nc.sbuf_tensor("TRI", [128, 128], F32))
    IDN = en(nc.sbuf_tensor("IDN", [128, 128], BF16))
    PAN = en(nc.sbuf_tensor("PAN", [128, 192], F32))
    PMSK = en(nc.sbuf_tensor("PMSK", [128, 1], F32))
    QSTG = XP[:, 1024:2048]
    QZ = en(nc.sbuf_tensor("QZ", [128, 1024], F32))
    QSG = en(nc.sbuf_tensor("QSG", [128, 1024], F32))
    QCB = en(nc.sbuf_tensor("QCB", [128, 1024], F32))
    QOUT = en(nc.sbuf_tensor("QOUT", [128, 1024], BF16))
    QPAN = en(nc.sbuf_tensor("QPAN", [128, 192], F32))
    HT = O32[:, 512:1024]
    JNK = XP[:, 0:512]
    SGT = KT[:]
    HQF = VT[:]
    WSTG = QT[:]

    PB = [en(nc.psum_tensor(f"PB{i}", [128, 512], F32)) for i in range(8)]
    PBbf = [PB[i].ap().bitcast(BF16) for i in range(8)]

    sems = {}
    for s in ("LD", "LW", "T", "V", "S", "ST", "SP", "SQ", "SD",
              "LDX", "LDP0", "LDP1", "LC", "W3A", "W3B", "LQ0", "LQ1", "LW4", "LH",
              "LX0", "LX1", "LZ0", "LZ1", "SQX"):
        sems[s] = en(nc.semaphore(s))

    def sy(fn, waits=(), inc="LD", mode="dma"):
        return kb.ev("sync", fn, waits, inc, mode)

    def gp(fn, waits=(), inc="LW", mode="dma"):
        return kb.ev("gpsimd", fn, waits, inc, mode)

    def tv(fn, waits=(), inc="T", mode="inst"):
        return kb.ev("tensor", fn, waits, inc, mode)

    def vv(fn, waits=(), inc="V", mode="inst"):
        return kb.ev("vector", fn, waits, inc, mode)

    def sc(fn, waits=(), inc="S", mode="inst"):
        return kb.ev("scalar", fn, waits, inc, mode)

    def ksl(buf, k, w=128):
        return buf[:, k * w:(k + 1) * w]

    def wsrc(d):
        return d.rearrange("(k p) n -> p k n", p=128)

    sy(lambda e: e.dma_start(out=TRI[:], in_=tri_d), inc="LC")
    sy(lambda e: e.dma_start(out=IDN[:], in_=idn_d), inc="LC")
    sy(lambda e: e.dma_start(out=ONW[:], in_=onw_d), inc="LC")
    ld_const = sy(lambda e: e.dma_start(out=WB1[:], in_=anw_d), inc="LC")
    ld_pmsk = sy(lambda e: e.dma_start(out=PMSK[:], in_=pmask_d), inc="LC")
    ld0 = ld_const
    gp(lambda e: e.dma_start(out=WA[:, 0 * NK * D:1 * NK * D].rearrange("p (k n) -> p k n", k=NK), in_=wsrc(wq_d)))
    gp(lambda e: e.dma_start(out=WA[:, 1 * NK * D:2 * NK * D].rearrange("p (k n) -> p k n", k=NK), in_=wsrc(wk_d)))
    lw_v = gp(lambda e: e.dma_start(out=WA[:, 2 * NK * D:3 * NK * D].rearrange("p (k n) -> p k n", k=NK), in_=wsrc(wv_d)))
    lw_o = gp(lambda e: e.dma_start(out=WA[:, 3 * NK * D:4 * NK * D].rearrange("p (k n) -> p k n", k=NK), in_=wsrc(wo_d)))
    WQs = WA[:, 0 * NK * D:1 * NK * D]
    WKs = WA[:, 1 * NK * D:2 * NK * D]
    WVs = WA[:, 2 * NK * D:3 * NK * D]
    WOs = WA[:, 3 * NK * D:4 * NK * D]

    v_setup = vv(lambda e: [e.memset(PAN[:], 0.0), e.memset(QPAN[:], 0.0), e.memset(S32[:], 0.0)][-1], mode="drain")
    sc(lambda e: e.preload_act_table(ACT.Sqrt), inc=None)

    qst = {"ssg": 0, "st": 0, "vqf": 0, "tcons": 0, "stSQ": 0, "stSQX": 0}

    def emit_quant_units(units, to_sbuf=False, store_sem="SQ"):
        n = len(units)
        lds = [0] * n
        ldsem = [None] * n
        for ui, u in enumerate(units):
            if u.get("src") is not None:
                ldsem[ui] = "LQ%d" % u["slot"]
                lds[ui] = sy(lambda e, u=u: e.dma_start(out=QSTG[:, u["slot"] * 512:u["slot"] * 512 + u["w"]],
                                                        in_=u["src"]),
                             waits=[("S", qst["ssg"]), ("V", qst.get("ldv", 0))], inc=ldsem[ui])
        ssg = qst["ssg"]
        for ui, u in enumerate(units):
            j, w = u["slot"], u["w"]
            src_sb = QSTG[:, j * 512:j * 512 + w] if u.get("src") is not None else u["sb"]
            waits = ([(ldsem[ui], lds[ui])] if u.get("src") is not None else [("V", u.get("vwait", 0))])
            waits = waits + [("V", qst["vqf"])]
            sc(lambda e, src_sb=src_sb, j=j, w=w: e.activation(out=QZ[:, j * 512:j * 512 + w], in_=src_sb, func=ACT.Abs),
               waits=waits, inc=None)
            ssg = sc(lambda e, src_sb=src_sb, j=j, w=w: e.sign(out=QSG[:, j * 512:j * 512 + w], in_=src_sb),
                     mode="drain")
        qst["ssg"] = ssg
        for ui, u in enumerate(units):
            j, w = u["slot"], u["w"]
            vv(lambda e, j=j, w=w: e.tensor_reduce(
                out=QPAN[:, j * 32:j * 32 + w // 16],
                in_=QZ[:, j * 512:j * 512 + w].rearrange("p (b k) -> p b k", k=16),
                axis=AX.X, op=ALU.max, apply_absolute_value=True),
               waits=[("S", ssg)], mode="drain" if ui == n - 1 else "inst")
        for ui, u in enumerate(units):
            j, w = u["slot"], u["w"]
            vv(lambda e, j=j, w=w: e.tensor_scalar(
                out=QPAN[:, 64 + j * 32:64 + j * 32 + w // 16], in0=QPAN[:, j * 32:j * 32 + w // 16],
                scalar1=1.0 / 12, scalar2=5e-13, op0=ALU.mult, op1=ALU.max),
               mode="drain" if ui == n - 1 else "inst")
        for ui, u in enumerate(units):
            j, w = u["slot"], u["w"]
            vv(lambda e, j=j, w=w: e.reciprocal(
                out=QPAN[:, 128 + j * 32:128 + j * 32 + w // 16], in_=QPAN[:, 64 + j * 32:64 + j * 32 + w // 16]),
               mode="drain" if ui == n - 1 else "inst")
        for u in units:
            j, w = u["slot"], u["w"]
            invb = QPAN[:, 128 + j * 32:128 + j * 32 + w // 16].unsqueeze(2).broadcast_to([128, w // 16, 16])
            vv(lambda e, j=j, w=w, invb=invb: e.scalar_tensor_tensor(
                out=QZ[:, j * 512:j * 512 + w].rearrange("p (b k) -> p b k", k=16),
                in0=QZ[:, j * 512:j * 512 + w].rearrange("p (b k) -> p b k", k=16),
                scalar=0.5, in1=invb, op0=ALU.mult, op1=ALU.mult))

        def chain(m, first=False):
            for u in units:
                j, w = u["slot"], u["w"]
                zj = QZ[:, j * 512:j * 512 + w]
                cj = QCB[:, j * 512:j * 512 + w]
                if first:
                    vv(lambda e, zj=zj, cj=cj, m=m: e.tensor_scalar(out=cj, in0=zj, scalar1=m, scalar2=None,
                                                                    op0=ALU.is_gt))
                else:
                    vv(lambda e, zj=zj, cj=cj, m=m: e.scalar_tensor_tensor(out=cj, in0=zj, scalar=m, in1=cj,
                                                                           op0=ALU.is_gt, op1=ALU.add))

        chain(2.5, True)
        chain(3.5)
        chain(5.0)
        chain(5.0)
        for u in units:
            j, w = u["slot"], u["w"]
            cj = QCB[:, j * 512:j * 512 + w]
            vv(lambda e, cj=cj: e.tensor_scalar(out=cj, in0=cj, scalar1=2.0, scalar2=None, op0=ALU.mult))
        chain(0.25)
        chain(0.75)
        chain(1.25)
        chain(1.75)
        for u in units:
            j, w = u["slot"], u["w"]
            sclb = QPAN[:, 64 + j * 32:64 + j * 32 + w // 16].unsqueeze(2).broadcast_to([128, w // 16, 16])
            vv(lambda e, j=j, w=w, sclb=sclb: e.tensor_tensor(
                out=QSG[:, j * 512:j * 512 + w].rearrange("p (b k) -> p b k", k=16),
                in0=QSG[:, j * 512:j * 512 + w].rearrange("p (b k) -> p b k", k=16), in1=sclb, op=ALU.mult))
        vqf = qst["vqf"]
        for ui, u in enumerate(units):
            j, w = u["slot"], u["w"]
            waits = ([(store_sem, qst["st" + store_sem])] if not to_sbuf else [("T", qst["tcons"])])
            vqf = vv(lambda e, j=j, w=w: e.tensor_tensor(
                out=QOUT[:, j * 512:j * 512 + w], in0=QCB[:, j * 512:j * 512 + w],
                in1=QSG[:, j * 512:j * 512 + w], op=ALU.mult),
                waits=waits if ui == 0 else [], mode="drain" if ui == n - 1 else "inst")
        qst["vqf"] = vqf
        if not to_sbuf:
            stq = qst["st" + store_sem]
            for u in units:
                j, w = u["slot"], u["w"]
                stq = sy(lambda e, u=u, j=j, w=w: e.dma_start(out=u["dst"], in_=QOUT[:, j * 512:j * 512 + w]),
                         waits=[("V", vqf)], inc=store_sem)
            qst["st" + store_sem] = stq
        return vqf

    ld_xo_all = sy(lambda e: e.dma_start(out=XN[:].rearrange("p (c n) -> p c n", c=NCH),
                                         in_=xo.rearrange("(c p) n -> p c n", p=128)), inc="LDX")
    s_sq = 0
    for i in range(NCH):
        vv(lambda e, i=i: e.reduce_sum(out=PAN[:, i:i + 1], in_=ksl(XN, i, D), axis=AX.X),
           waits=[("LDX", ld_xo_all), ("V", v_setup)])
        s_sq = sc(lambda e, i=i: e.activation(out=O32[:], in_=ksl(XN, i, D), func=ACT.Square,
                                              accum_out=PAN[:, 16 + i:17 + i]),
                  waits=[("LDX", ld_xo_all), ("V", v_setup)], mode="drain")
    vv(lambda e: e.tensor_scalar(out=PAN[:, 32:48], in0=PAN[:, 0:16], scalar1=1.0 / D, scalar2=None, op0=ALU.mult),
       waits=[("S", s_sq)], mode="drain")
    vv(lambda e: e.tensor_scalar(out=PAN[:, 144:160], in0=PAN[:, 16:32], scalar1=1.0 / D, scalar2=EPS,
                                 op0=ALU.mult, op1=ALU.add), mode="drain")
    vv(lambda e: e.tensor_tensor(out=PAN[:, 160:176], in0=PAN[:, 32:48], in1=PAN[:, 32:48], op=ALU.mult), mode="drain")
    vv(lambda e: e.tensor_tensor(out=PAN[:, 144:160], in0=PAN[:, 144:160], in1=PAN[:, 160:176], op=ALU.subtract),
       mode="drain")
    v_tr = vv(lambda e: e.reciprocal(out=PAN[:, 160:176], in_=PAN[:, 144:160]), mode="drain")
    s_tr = sc(lambda e: e.sqrt(out=PAN[:, 48:64], in_=PAN[:, 160:176]), waits=[("V", v_tr)], mode="drain")
    v_ln = [0] * NCH
    s_rms = 0
    for i in range(NCH):
        v_ln[i] = vv(lambda e, i=i: e.tensor_scalar(out=ksl(XN, i, D), in0=ksl(XN, i, D),
                                                    scalar1=PAN[:, 32 + i:33 + i], scalar2=PAN[:, 48 + i:49 + i],
                                                    op0=ALU.subtract, op1=ALU.mult),
                     waits=[("S", s_tr)], mode="drain")
        s_rms = sc(lambda e, i=i: e.activation(out=O32[:], in_=ksl(XN, i, D), func=ACT.Square,
                                               accum_out=PAN[:, 64 + i:65 + i]),
                   waits=[("V", v_ln[i])], mode="drain")
    st_xn_spill = sy(lambda e: e.dma_start(out=xns_d.rearrange("(c p) n -> p c n", p=128),
                                           in_=XN[:].rearrange("p (c n) -> p c n", c=NCH)),
                     waits=[("V", v_ln[NCH - 1])], inc="SP")
    if "xn" in dbg:
        sy(lambda e: e.dma_start(out=dbg_outs["xn"].rearrange("(c p) n -> p c n", p=128),
                                 in_=XN[:].rearrange("p (c n) -> p c n", c=NCH)),
           waits=[("V", v_ln[NCH - 1])], inc="SP")
        st_xn_spill = kb.now("SP")
    sd_xn = kb.now("SD")
    vv(lambda e: e.tensor_scalar(out=PAN[:, 144:160], in0=PAN[:, 64:80], scalar1=1.0 / D, scalar2=EPS,
                                 op0=ALU.mult, op1=ALU.add),
       waits=[("S", s_rms)], mode="drain")
    v_tr = vv(lambda e: e.reciprocal(out=PAN[:, 160:176], in_=PAN[:, 144:160]), mode="drain")
    s_rstd1 = sc(lambda e: e.sqrt(out=PAN[:, 80:96], in_=PAN[:, 160:176]), waits=[("V", v_tr)], mode="drain")

    xq_ssg = [0] * NCH

    def emit_xq(i):
        emit_quant_units([
            dict(sb=ksl(XN, i, D)[:, 0:512], vwait=v_ln[i],
                 dst=xqs_d[i * 128:(i + 1) * 128, 0:512], w=512, slot=0),
            dict(sb=ksl(XN, i, D)[:, 512:1024], vwait=v_ln[i],
                 dst=xqs_d[i * 128:(i + 1) * 128, 512:1024], w=512, slot=1),
        ], store_sem="SQX")
        xq_ssg[i] = qst["ssg"]

    wq_units = []
    for (src_d, dst_dq, rows, cols) in ((fg_d, fgq_d, I3, D), (fu_d, fuq_d, I3, D)):
        for rb in range(rows // 128):
            for cb in range(cols // 512):
                wq_units.append(dict(src=src_d[rb * 128:(rb + 1) * 128, cb * 512:(cb + 1) * 512],
                                     dst=dst_dq[rb * 128:(rb + 1) * 128, cb * 512:(cb + 1) * 512], w=512))
    for rb in range(D // 128):
        for cb in range((I3 + 511) // 512):
            w = min(512, I3 - cb * 512)
            wq_units.append(dict(src=fd_d[rb * 128:(rb + 1) * 128, cb * 512:cb * 512 + w],
                                 dst=fdq_d[rb * 128:(rb + 1) * 128, cb * 512:cb * 512 + w], w=w))
    qi = [0]

    def emit_wq(npairs):
        for _ in range(npairs):
            if qi[0] >= len(wq_units):
                return
            batch = wq_units[qi[0]:qi[0] + 2]
            qi[0] += len(batch)
            for j, u in enumerate(batch):
                u["slot"] = j
            emit_quant_units(batch)

    t_h1tr = 0
    v_free_pb0 = 0
    v_free_kv = 0
    t_sprev = 0
    ld_xp = [0] * NCH
    ld_xp[0] = sy(lambda e: e.dma_start(out=XP[:, 0:D], in_=xp[0:128, :]), inc="LDP0")
    ld_xp[1] = sy(lambda e: e.dma_start(out=XP[:, D:2 * D], in_=xp[128:256, :]), inc="LDP1")
    for i in range(NCH):
        pg = i % 2
        xpc = XP[:, pg * D:(pg + 1) * D]
        if "XP0" in dbg and i == 0:
            xp0_gate = sy(lambda e: e.dma_start(out=dbg_outs["XP0"][:, 0:D], in_=XP[:, 0:D]),
                          waits=[("LDP0", ld_xp[0])], inc="SD")
            kb.ev("vector", None, waits=[("SD", xp0_gate)], inc=None)
        v1 = vv(lambda e, xpc=xpc: e.reduce_sum(out=PAN[:, 176:177], in_=xpc, axis=AX.X),
                waits=[("LDP%d" % (i % 2), ld_xp[i])], mode="drain")
        s1 = sc(lambda e, xpc=xpc: e.activation(out=O32[:], in_=xpc, func=ACT.Square,
                                                accum_out=PAN[:, 177:178]),
                waits=[("LDP%d" % (i % 2), ld_xp[i]), ("V", v_setup)], mode="drain")
        vv(lambda e: e.tensor_scalar(out=PAN[:, 178:180], in0=PAN[:, 176:178], scalar1=1.0 / D, scalar2=None,
                                     op0=ALU.mult), waits=[("S", s1), ("V", v1)], mode="drain")
        vv(lambda e: e.tensor_tensor(out=PAN[:, 180:181], in0=PAN[:, 178:179], in1=PAN[:, 178:179], op=ALU.mult),
           mode="drain")
        vv(lambda e: e.tensor_tensor(out=PAN[:, 181:182], in0=PAN[:, 179:180], in1=PAN[:, 180:181], op=ALU.subtract),
           mode="drain")
        vv(lambda e: e.tensor_scalar(out=PAN[:, 181:182], in0=PAN[:, 181:182], scalar1=EPS, scalar2=None, op0=ALU.add),
           mode="drain")
        v1 = vv(lambda e: e.reciprocal(out=PAN[:, 182:183], in_=PAN[:, 181:182]), mode="drain")
        s1 = sc(lambda e: e.sqrt(out=PAN[:, 183:184], in_=PAN[:, 182:183]), waits=[("V", v1)], mode="drain")
        v1 = vv(lambda e, xpc=xpc: e.tensor_scalar(out=xpc, in0=xpc, scalar1=PAN[:, 178:179], scalar2=PAN[:, 183:184],
                                                   op0=ALU.subtract, op1=ALU.mult),
                waits=[("S", s1)], mode="drain")
        s1 = sc(lambda e, xpc=xpc: e.activation(out=O32[:], in_=xpc, func=ACT.Square,
                                                accum_out=PAN[:, 185:186]), waits=[("V", v1)], mode="drain")
        v1 = vv(lambda e: e.tensor_scalar(out=PAN[:, 177:178], in0=PAN[:, 185:186], scalar1=1.0 / D, scalar2=EPS,
                                          op0=ALU.mult, op1=ALU.add), waits=[("S", s1)], mode="drain")
        v1 = vv(lambda e: e.reciprocal(out=PAN[:, 178:179], in_=PAN[:, 177:178]), mode="drain")
        s1 = sc(lambda e: e.sqrt(out=PAN[:, 179:180], in_=PAN[:, 178:179]), waits=[("V", v1)], mode="drain")
        v_h1 = vv(lambda e, xpc=xpc: e.scalar_tensor_tensor(out=H1T[:], in0=xpc, scalar=PAN[:, 179:180], in1=WB1[:],
                                                            op0=ALU.mult, op1=ALU.mult),
                  waits=[("S", s1), ("LC", ld_const), ("T", t_h1tr)], mode="drain")
        if i + 2 < NCH:
            ld_xp[i + 2] = sy(lambda e, i=i, pg=pg: e.dma_start(out=XP[:, pg * D:(pg + 1) * D],
                                                                in_=xp[(i + 2) * 128:(i + 3) * 128, :]),
                              waits=[("V", v_h1)], inc="LDP%d" % pg)
        t1 = 0
        for k in range(NK):
            t1 = tv(lambda e, k=k: e.transpose(ksl(PBbf[0], k), ksl(H1T, k), IDN[:]),
                    waits=([("V", v_h1), ("V", v_free_pb0), ("LC", ld0), ("LW", lw_o)] if k == 0 else []))
        t_h1tr = t1
        v_h1f = vv(lambda e: e.tensor_copy(out=H1F[:], in_=PBbf[0][:]), waits=[("T", t1)], mode="inst")
        v_free_pb0 = v_h1f
        tk_ = 0
        for half in range(2):
            for k in range(NK):
                tk_ = tv(lambda e, half=half, k=k: e.matmul(PB[1 + half][:], ksl(H1F, k),
                                                            WKs[:, k * D + half * 512:k * D + half * 512 + 512],
                                                            start=(k == 0), stop=(k == NK - 1)),
                         waits=([("V", v_h1f), ("V", v_free_kv)] if (half == 0 and k == 0) else []))
        for half in range(2):
            for k in range(NK):
                tk_ = tv(lambda e, half=half, k=k: e.matmul(PB[3 + half][:], ksl(H1F, k),
                                                            WVs[:, k * D + half * 512:k * D + half * 512 + 512],
                                                            start=(k == 0), stop=(k == NK - 1)))
        v_kt = vv(lambda e: [e.tensor_copy(out=KT[:, 0:512], in_=PB[1][:]),
                             e.tensor_copy(out=KT[:, 512:1024], in_=PB[2][:]),
                             e.tensor_copy(out=VT[:, 0:512], in_=PB[3][:]),
                             e.tensor_copy(out=VT[:, 512:1024], in_=PB[4][:])][-1],
                  waits=[("T", tk_)], mode="inst")
        v_free_kv = v_kt
        for h in range(H):
            bank, off = (6 + h // 4), (h % 4) * 128
            t_sprev = tv(lambda e, h=h, bank=bank, off=off: e.matmul(
                PB[bank][:, off:off + 128], ksl(KT, h), ksl(VT, h),
                start=(h % 4 == 0), stop=(h % 4 == 3), skip_group_check=True),
                waits=([("V", v_kt)] + ([("V", v_supd_p)] if i > 0 else [("V", v_setup)])) if h == 0 else [])
        v_supd_p = vv(lambda e: [e.tensor_tensor(out=S32[:, 0:512], in0=S32[:, 0:512], in1=PB[6][:], op=ALU.add),
                                 e.tensor_tensor(out=S32[:, 512:1024], in0=S32[:, 512:1024], in1=PB[7][:],
                                                 op=ALU.add)][-1],
                      waits=[("T", t_sprev)], mode="inst")
        if "P0" in dbg and i == 0:
            sy(lambda e: e.dma_start(out=dbg_outs["P0"][:, 0:D].bitcast(BF16)[:, 0:D], in_=KT[:]),
               waits=[("V", v_supd_p)], inc="SD")
            sy(lambda e: e.dma_start(out=dbg_outs["P0"][:, D:2 * D].bitcast(BF16)[:, 0:D], in_=VT[:]), inc="SD")
            sy(lambda e: e.dma_start(out=dbg_outs["P0"][:, 2 * D:3 * D], in_=S32[:]), inc="SD")
            p0_gate = sy(lambda e: e.dma_start(out=dbg_outs["P0"][:, 3 * D:4 * D].bitcast(BF16)[:, 0:D], in_=H1F[:]),
                         inc="SD")
            kb.ev("vector", None, waits=[("SD", p0_gate)], inc=None)
        if "PFX" in dbg and i == NCH - 1:
            sy(lambda e, xpc=xpc: e.dma_start(out=dbg_outs["PFX"][:, 0:D], in_=xpc),
               waits=[("V", v_kt)], inc="SD")
            sy(lambda e: e.dma_start(out=dbg_outs["PFX"][:, D:2 * D].bitcast(BF16)[:, 0:D], in_=H1T[:]), inc="SD")
            sy(lambda e: e.dma_start(out=dbg_outs["PFX"][:, 2 * D:3 * D].bitcast(BF16)[:, 0:D], in_=KT[:]), inc="SD")
            pfx_gate = sy(lambda e: e.dma_start(out=dbg_outs["PFX"][:, 3 * D:4 * D].bitcast(BF16)[:, 0:D], in_=VT[:]), inc="SD")
            kb.ev("vector", None, waits=[("SD", pfx_gate)], inc=None)
        emit_xq(i)
    v_msk = vv(lambda e: e.tensor_scalar(out=S32[:], in0=S32[:], scalar1=PMSK[:], scalar2=None, op0=ALU.mult),
               waits=[("V", v_supd_p), ("LC", ld_pmsk)], mode="drain")
    v_sbf = vv(lambda e: e.tensor_copy(out=SBF[:], in_=S32[:]), mode="inst")
    dbg_s_gate = 0
    if "S" in dbg:
        dbg_s_gate = sy(lambda e: e.dma_start(out=dbg_outs["S"], in_=S32[:]), waits=[("V", v_sbf)], inc="SD")

    qst["ldv"] = kb.now("V")
    v_x2 = [0] * NCH
    v_oev_prev = 0
    s_h2_prev = 0
    s_or_prev = 0
    v_supd_prev = 0
    s_h2_last = 0
    for i in range(NCH):
        v_h1 = vv(lambda e, i=i: e.scalar_tensor_tensor(out=H1T[:], in0=ksl(XN, i, D), scalar=PAN[:, 80 + i:81 + i],
                                                        in1=WB1[:], op0=ALU.mult, op1=ALU.mult),
                  waits=[("S", s_rstd1), ("V", v_sbf), ("T", t_h1tr), ("S", xq_ssg[i])], mode="drain")
        t1 = 0
        for k in range(NK):
            t1 = tv(lambda e, k=k: e.transpose(ksl(PBbf[0], k), ksl(H1T, k), IDN[:]),
                    waits=([("V", v_h1), ("V", v_free_pb0)] if k == 0 else []))
        t_h1tr = t1
        v_h1f = vv(lambda e: e.tensor_copy(out=H1F[:], in_=PBbf[0][:]), waits=[("T", t1)], mode="inst")
        v_free_pb0 = v_h1f

        def proj(Wsb, dst, v_ready):
            tlast = 0
            for half in range(2):
                for k in range(NK):
                    tlast = tv(lambda e, half=half, k=k, Wsb=Wsb: e.matmul(
                        PB[1 + half][:], ksl(H1F, k), Wsb[:, k * D + half * 512:k * D + half * 512 + 512],
                        start=(k == 0), stop=(k == NK - 1)),
                        waits=([("V", v_ready)] if (half == 0 and k == 0) else []))
            return tlast, vv(lambda e, dst=dst: [e.tensor_copy(out=dst[:, 0:512], in_=PB[1][:]),
                                                 e.tensor_copy(out=dst[:, 512:1024], in_=PB[2][:])][-1],
                             waits=[("T", tlast)], mode="inst")

        _, v_q = proj(WQs, QT, v_h1f)
        _, v_k = proj(WKs, KT, v_q)
        _, v_v = proj(WVs, VT, v_k)
        t1 = 0
        for k in range(NK):
            t1 = tv(lambda e, k=k: e.transpose(ksl(PBbf[0], k), ksl(QT, k), IDN[:]),
                    waits=([("V", v_q), ("V", v_free_pb0)] if k == 0 else []))
        v_qf = vv(lambda e: e.tensor_copy(out=QF[:], in_=PBbf[0][:]), waits=[("T", t1)], mode="inst")
        t1 = 0
        for k in range(NK):
            t1 = tv(lambda e, k=k: e.transpose(ksl(PBbf[0], k), ksl(KT, k), IDN[:]),
                    waits=([("V", v_k), ("V", v_qf)] if k == 0 else []))
        v_kf = vv(lambda e: e.tensor_copy(out=KF[:], in_=PBbf[0][:]), waits=[("T", t1)], mode="inst")
        v_free_pb0 = v_kf
        v_atn = {}
        t_o = [0] * H
        t_su = [0] * H
        for h in range(H):
            abank = 3 if h % 2 == 0 else 0
            t_at = tv(lambda e, h=h, abank=abank: e.matmul(PB[abank][:, 0:128], ksl(KF, h), ksl(QF, h),
                                                           start=True, stop=True),
                      waits=([("V", v_kf), ("V", v_v)] if h == 0 else
                             ([("V", v_kf)] if h == 1 else [("V", v_atn[h - 2])])))
            v_atn[h] = vv(lambda e, h=h, abank=abank: e.tensor_tensor(out=ATN[:, (h % 2) * 128:(h % 2) * 128 + 128],
                                                                      in0=PB[abank][:, 0:128], in1=TRI[:], op=ALU.mult),
                          waits=[("T", t_at)], mode="drain")
            bank, off = 4 + h // 4, (h % 4) * 128
            t_o[h] = tv(lambda e, h=h, bank=bank, off=off: [
                e.matmul(PB[bank][:, off:off + 128], ATN[:, (h % 2) * 128:(h % 2) * 128 + 128], ksl(VT, h),
                         start=(h % 4 == 0), stop=False, skip_group_check=True),
                e.matmul(PB[bank][:, off:off + 128], ksl(QF, h), ksl(SBF, h), start=False,
                         stop=(h % 4 == 3), skip_group_check=True)][-1],
                waits=[("V", v_atn[h])] + ([("V", v_oev_prev)] if h == 0 else []))
            bank2, off2 = 6 + h // 4, (h % 4) * 128
            t_su[h] = tv(lambda e, h=h, bank2=bank2, off2=off2: e.matmul(
                PB[bank2][:, off2:off2 + 128], ksl(KT, h), ksl(VT, h),
                start=(h % 4 == 0), stop=(h % 4 == 3), skip_group_check=True),
                waits=(([("V", v_sbf if i == 0 else v_supd_prev)] +
                        ([("SD", dbg_s_gate)] if i == 0 else [])) if h == 0 else []))
        v_oev = vv(lambda e: [e.tensor_copy(out=O32[:, 0:512], in_=PB[4][:]),
                              e.tensor_copy(out=O32[:, 512:1024], in_=PB[5][:])][-1],
                   waits=[("T", t_o[H - 1]), ("S", s_h2_prev)], mode="inst")
        v_oev_prev = v_oev
        s_oss = 0
        for h in range(H):
            s_oss = sc(lambda e, h=h: e.activation(out=JNK[:, 0:128], in_=ksl(O32, h), func=ACT.Square,
                                                   accum_out=PAN[:, 128 + h:129 + h]),
                       waits=([("V", v_oev)] if h == 0 else []), mode="drain" if h == H - 1 else "inst")
        vv(lambda e: e.tensor_scalar(out=PAN[:, 144:152], in0=PAN[:, 128:136], scalar1=1.0 / 128, scalar2=EPS,
                                     op0=ALU.mult, op1=ALU.add), waits=[("S", s_oss)], mode="drain")
        v1 = vv(lambda e: e.reciprocal(out=PAN[:, 152:160], in_=PAN[:, 144:152]), mode="drain")
        s_or = sc(lambda e: e.sqrt(out=PAN[:, 136:144], in_=PAN[:, 152:160]), waits=[("V", v1)], mode="drain")
        s_or_prev = s_or
        v_on = 0
        for h in range(H):
            v_on = vv(lambda e, h=h: e.scalar_tensor_tensor(out=ksl(ONF, h), in0=ksl(O32, h),
                                                            scalar=PAN[:, 136 + h:137 + h], in1=ONW[:],
                                                            op0=ALU.mult, op1=ALU.mult),
                      waits=([("S", s_or)] if h == 0 else []), mode="drain" if h == H - 1 else "inst")
        t1 = 0
        for h in range(H):
            t1 = tv(lambda e, h=h: e.transpose(ksl(PBbf[0], h), ksl(ONF, h), IDN[:]),
                    waits=([("V", v_on), ("V", v_atn.get(7, 0))] if h == 0 else []))
        v_onf = vv(lambda e: e.tensor_copy(out=H1F[:], in_=PBbf[0][:]), waits=[("T", t1)], mode="inst")
        v_free_pb0 = v_onf
        t_wo = 0
        for half in range(2):
            for h in range(H):
                t_wo = tv(lambda e, half=half, h=h: e.matmul(
                    PB[1 + half][:], ksl(H1F, h), WOs[:, h * D + half * 512:h * D + half * 512 + 512],
                    start=(h == 0), stop=(h == H - 1)),
                    waits=([("V", v_onf), ("LW", lw_o)] if (half == 0 and h == 0) else []))
        v_x2[i] = vv(lambda e, i=i: [e.tensor_tensor(out=ksl(XN, i, D)[:, 0:512], in0=ksl(XN, i, D)[:, 0:512],
                                                     in1=PB[1][:], op=ALU.add),
                                     e.tensor_tensor(out=ksl(XN, i, D)[:, 512:1024], in0=ksl(XN, i, D)[:, 512:1024],
                                                     in1=PB[2][:], op=ALU.add)][-1],
                     waits=[("T", t_wo)] + ([("SP", st_xn_spill)] if i == 0 else []), mode="inst")
        vv(lambda e: [e.tensor_tensor(out=S32[:, 0:512], in0=S32[:, 0:512], in1=PB[6][:], op=ALU.add),
                      e.tensor_tensor(out=S32[:, 512:1024], in0=S32[:, 512:1024], in1=PB[7][:], op=ALU.add)][-1],
           waits=[("T", t_su[H - 1])])
        v_supd_prev = vv(lambda e: e.tensor_copy(out=SBF[:], in_=S32[:]), mode="inst")
        s_h2_last = sc(lambda e, i=i: e.activation(out=O32[:], in_=ksl(XN, i, D), func=ACT.Square,
                                                   accum_out=PAN[:, 96 + i:97 + i]),
                       waits=[("V", v_x2[i])], mode="drain")
        s_h2_prev = s_h2_last
        if "MIX" in dbg and i == 0:
            sy(lambda e: e.dma_start(out=dbg_outs["MIX"][:, 0:D].bitcast(BF16)[:, 0:D], in_=QT[:]),
               waits=[("V", v_on)], inc="SD")
            sy(lambda e: e.dma_start(out=dbg_outs["MIX"][:, D:2 * D].bitcast(BF16)[:, 0:D], in_=KT[:]), inc="SD")
            sy(lambda e: e.dma_start(out=dbg_outs["MIX"][:, 2 * D:3 * D].bitcast(BF16)[:, 0:D], in_=VT[:]), inc="SD")
            sy(lambda e: e.dma_start(out=dbg_outs["MIX"][:, 3 * D:4 * D], in_=O32[:]), inc="SD")
            sy(lambda e: e.dma_start(out=dbg_outs["MIX"][:, 4 * D:5 * D].bitcast(BF16)[:, 0:D], in_=ONF[:]), inc="SD")
            dbg_gate = sy(lambda e: e.dma_start(out=dbg_outs["MIX"][:, 5 * D:5 * D + 192], in_=PAN[:]), inc="SD")
            kb.ev("vector", None, waits=[("SD", dbg_gate)], inc=None)

    dbg_x2_gate = 0
    if "x2" in dbg:
        dbg_x2_gate = sy(lambda e: e.dma_start(out=dbg_outs["x2"].rearrange("(c p) n -> p c n", p=128),
                                               in_=XN[:].rearrange("p (c n) -> p c n", c=NCH)),
                         waits=[("V", v_x2[NCH - 1])], inc="SD")

    t_after_s2_main = kb.now("T")
    ld_mnw = sy(lambda e: e.dma_start(out=WB1[:], in_=mnw_d), waits=[("V", kb.now("V"))], inc="LC")
    vv(lambda e: e.tensor_scalar(out=PAN[:, 144:160], in0=PAN[:, 96:112], scalar1=1.0 / D, scalar2=EPS,
                                 op0=ALU.mult, op1=ALU.add), waits=[("S", s_h2_last)], mode="drain")
    v1 = vv(lambda e: e.reciprocal(out=PAN[:, 160:176], in_=PAN[:, 144:160]), mode="drain")
    s_rstd2 = sc(lambda e: e.sqrt(out=PAN[:, 112:128], in_=PAN[:, 160:176]), waits=[("V", v1)], mode="drain")
    v_h2f = [0] * NCH
    for i in range(NCH):
        v_h2 = vv(lambda e, i=i: e.scalar_tensor_tensor(out=H1T[:], in0=ksl(XN, i, D), scalar=PAN[:, 112 + i:113 + i],
                                                        in1=WB1[:], op0=ALU.mult, op1=ALU.mult),
                  waits=[("S", s_rstd2), ("LC", ld_mnw), ("T", t_h1tr)], mode="drain")
        t1 = 0
        for k in range(NK):
            t1 = tv(lambda e, k=k: e.transpose(ksl(PBbf[0], k), ksl(H1T, k), IDN[:]),
                    waits=([("V", v_h2), ("V", v_free_pb0)] if k == 0 else []))
        t_h1tr = t1
        v_h2f[i] = vv(lambda e, i=i: e.tensor_copy(out=ksl(H2F, i, D), in_=PBbf[0][:]), waits=[("T", t1)], mode="inst")
        v_free_pb0 = v_h2f[i]
        emit_wq(1)

    sc(lambda e: e.preload_act_table(ACT.Silu), waits=[("S", kb.now("S"))], inc=None)
    NT2 = I2 // 512

    def slot(s, pg):
        return WA[:, (pg * 3 + s) * 4096:(pg * 3 + s + 1) * 4096]

    lw_t = [0] * NT2
    t_after_s2 = kb.now("T")
    t_tile_done = {0: t_after_s2, 1: t_after_s2}
    v_free_d = 0
    v_s_done = 0
    v_free_gp3 = [0, 0]
    t_a2cons = [0, 0]
    pend3 = [None]
    for it in range(NT2):
        pg = it % 2
        w_wait = [("T", t_tile_done[pg])]
        wsem = "W3A" if pg == 0 else "W3B"
        gp(lambda e, it=it, pg=pg: e.dma_start(
            out=slot(0, pg).rearrange("p (k n) -> p k n", k=NK),
            in_=wsrc(wg_d)[:, :, it * 512:(it + 1) * 512]), waits=w_wait, inc=wsem)
        gp(lambda e, it=it, pg=pg: e.dma_start(
            out=slot(1, pg).rearrange("p (k n) -> p k n", k=NK),
            in_=wsrc(wu_d)[:, :, it * 512:(it + 1) * 512]), inc=wsem)
        lw_t[it] = gp(lambda e, it=it, pg=pg: e.dma_start(
            out=slot(2, pg).rearrange("p (k n) -> p k n", k=4),
            in_=wd_d.rearrange("(k p) n -> p k n", p=128)[:, it * 4:(it + 1) * 4, :]), inc=wsem)
        lw_sem = wsem

        def emit_tail3(p):
            nonlocal v_free_pb0, v_free_d, v_s_done
            (i_, v_a2_, pg_, a2b_) = p
            t1 = 0
            for k in range(4):
                t1 = tv(lambda e, k=k, a2b_=a2b_: e.transpose(ksl(PBbf[0], k),
                                                              a2b_[:, k * 128:(k + 1) * 128], IDN[:]),
                        waits=([("V", v_a2_), ("V", v_free_pb0)] if k == 0 else []))
            t_a2cons[i_ % 2] = t1
            v_a2f = vv(lambda e: e.tensor_copy(out=HQF[:, 0:512], in_=PBbf[0][:, 0:512]),
                       waits=[("T", t1)], mode="inst")
            v_free_pb0 = v_a2f
            td = 0
            for half in range(2):
                for k in range(4):
                    td = tv(lambda e, pg_=pg_, half=half, k=k: e.matmul(
                        PB[3 + half][:], ksl(HQF, k),
                        slot(2, pg_)[:, k * 1024 + half * 512:k * 1024 + half * 512 + 512],
                        start=(k == 0), stop=(k == 3)),
                        waits=([("V", v_a2f), ("V", v_free_d)] if (half == 0 and k == 0) else []))
            last = (pg_ == (NT2 - 1) % 2 and i_ == NCH - 1 and it == NT2 - 1)
            v_free_d = vv(lambda e, i_=i_: [e.tensor_tensor(out=ksl(XN, i_, D)[:, 0:512],
                                                            in0=ksl(XN, i_, D)[:, 0:512],
                                                            in1=PB[3][:], op=ALU.add),
                                            e.tensor_tensor(out=ksl(XN, i_, D)[:, 512:1024],
                                                            in0=ksl(XN, i_, D)[:, 512:1024],
                                                            in1=PB[4][:], op=ALU.add)][-1],
                          waits=[("T", td)] + ([("SD", dbg_x2_gate)] if (it == 0 and i_ == 0) else []),
                          mode="drain" if last else "inst")
            if last:
                v_s_done = v_free_d

        for i in range(NCH):
            par = i % 2
            gb, ub = (1, 2) if par == 0 else (5, 6)
            silb, a2b = (SGT[:, 0:512], SGT[:, 512:1024]) if par == 0 else (H1T[:, 0:512], H1T[:, 512:1024])
            tg = 0
            for k in range(NK):
                tg = tv(lambda e, pg=pg, i=i, k=k, gb=gb: e.matmul(
                    PB[gb][:], H2F[:, i * D + k * 128:i * D + (k + 1) * 128],
                    slot(0, pg)[:, k * 512:(k + 1) * 512], start=(k == 0), stop=(k == NK - 1)),
                    waits=([("W3A" if pg == 0 else "W3B", lw_t[it]), ("V", v_h2f[i]),
                            ("V", v_free_gp3[par])] if k == 0 else []))
            tu = 0
            for k in range(NK):
                tu = tv(lambda e, pg=pg, i=i, k=k, ub=ub: e.matmul(
                    PB[ub][:], H2F[:, i * D + k * 128:i * D + (k + 1) * 128],
                    slot(1, pg)[:, k * 512:(k + 1) * 512], start=(k == 0), stop=(k == NK - 1)))
            if pend3[0] is not None:
                emit_tail3(pend3[0])
                pend3[0] = None
            s_sg = sc(lambda e, gb=gb, silb=silb: e.activation(out=silb, in_=PB[gb][:], func=ACT.Silu),
                      waits=[("T", tu), ("V", v_free_gp3[par])], mode="drain")
            v_a2 = vv(lambda e, ub=ub, silb=silb, a2b=a2b: e.tensor_tensor(out=a2b, in0=silb, in1=PB[ub][:],
                                                                           op=ALU.mult),
                      waits=[("S", s_sg), ("T", t_a2cons[par])], mode="inst")
            v_free_gp3[par] = v_a2
            pend3[0] = (i, v_a2, pg, a2b)
            if i % 2 == 1:
                emit_wq(1)
        if pend3[0] is not None:
            emit_tail3(pend3[0])
            pend3[0] = None
        t_tile_done[pg] = kb.now("T")
    emit_wq(200)
    sy(lambda e: e.dma_start(out=ss_d.rearrange("(c p) n -> p c n", p=128),
                             in_=XN[:].rearrange("p (c n) -> p c n", c=NCH)),
       waits=[("V", v_s_done)], inc="SP")
    if "s" in dbg:
        sy(lambda e: e.dma_start(out=dbg_outs["s"].rearrange("(c p) n -> p c n", p=128),
                                 in_=XN[:].rearrange("p (c n) -> p c n", c=NCH)),
           waits=[("V", v_s_done)], inc="SD")
    st_spill_s = kb.now("SP")
    sd_spill_s = kb.now("SD")

    all_q_stores = kb.now("SQX")
    all_wq_stores = kb.now("SQ")
    if "WQ" in dbg:
        sy(lambda e: e.dma_start(out=dbg_outs["WQ"], in_=fgq_d), waits=[("SQ", all_q_stores)], inc="SD")
        sy(lambda e: e.dma_start(out=dbg_outs["WD"], in_=fdq_d), inc="SD")
        sy(lambda e: e.dma_start(out=dbg_outs["FU"], in_=fuq_d), inc="SD")
        sy(lambda e: e.dma_start(out=dbg_outs["XQ"], in_=xqs_d), inc="SD")
    v_xqf = [0] * NCH
    t_xq_tr = 0
    t_after_s3 = kb.now("T")
    for i in range(NCH):
        ldq = sy(lambda e, i=i: e.dma_start(out=H1T[:], in_=xqs_d[i * 128:(i + 1) * 128, :]),
                 waits=[("SQX", all_q_stores), ("T", t_after_s3)] if i == 0 else [("T", t_xq_tr)], inc="LH")
        t1 = 0
        for k in range(NK):
            t1 = tv(lambda e, k=k: e.transpose(ksl(PBbf[0], k), ksl(H1T, k), IDN[:]),
                    waits=([("LH", ldq), ("V", v_free_pb0), ("V", v_s_done)] if k == 0 else []))
        t_xq_tr = t1
        v_xqf[i] = vv(lambda e, i=i: e.tensor_copy(out=ksl(H2F, i, D), in_=PBbf[0][:]), waits=[("T", t1)], mode="inst")
        v_free_pb0 = v_xqf[i]

    IT3 = [(0, 512), (512, 512), (1024, 512), (1536, 512), (2048, 512), (2560, 256)]

    def slot4(s, pg):
        return WA[:, (pg * 3 + s) * 4096:(pg * 3 + s + 1) * 4096]

    t_after_s3 = kb.now("T")
    t_tile4_done = {0: t_after_s3, 1: t_after_s3}
    v_free_g = 0
    v_free_d = 0
    v_gacc = [0] * NCH
    t_wtr = t_after_s3
    for ti, (i0, itw) in enumerate(IT3):
        pg = ti % 2
        nI = (itw + 127) // 128
        v_wslot = 0
        for wslot, src_q in ((0, fgq_d), (1, fuq_d)):
            for c in range(nI):
                ldw = sy(lambda e, src_q=src_q, i0=i0, c=c: e.dma_start(
                    out=WSTG[:], in_=src_q[i0 + c * 128:i0 + (c + 1) * 128, :]),
                    waits=[("T", t_wtr), ("T", t_tile4_done[pg]), ("SQ", all_wq_stores)], inc="LW4")
                if "WQ" in dbg and ti == 0 and wslot == 1 and c == nI - 1:
                    ws3g = sy(lambda e: e.dma_start(out=dbg_outs["WS3"], in_=WSTG[:, 0:D]),
                              waits=[("LW4", ldw)], inc="SD")
                    kb.ev("tensor", None, waits=[("SD", ws3g)], inc=None)
                t1 = 0
                for k in range(NK):
                    t1 = tv(lambda e, k=k: e.transpose(ksl(PBbf[0], k), ksl(WSTG, k), IDN[:]),
                            waits=([("LW4", ldw), ("V", v_free_pb0)] if k == 0 else []))
                t_wtr = t1
                v_wslot = vv(lambda e, wslot=wslot, pg=pg, c=c, itw=itw: [
                    e.tensor_copy(out=slot4(wslot, pg)[:, k * itw + c * 128:k * itw + (c + 1) * 128],
                                  in_=ksl(PBbf[0], k)) for k in range(NK)][-1],
                    waits=[("T", t1)], mode="drain")
                v_free_pb0 = v_wslot
        for hb in range(NK):
            ldw = sy(lambda e, hb=hb, i0=i0, itw=itw: e.dma_start(
                out=WSTG[:, 0:itw], in_=fdq_d[hb * 128:(hb + 1) * 128, i0:i0 + itw]),
                waits=[("T", t_wtr), ("SQ", all_wq_stores)], inc="LW4")
            t1 = 0
            for c in range(nI):
                t1 = tv(lambda e, c=c: e.transpose(ksl(PBbf[0], c), ksl(WSTG, c), IDN[:]),
                        waits=([("LW4", ldw), ("V", v_free_pb0)] if c == 0 else []))
            t_wtr = t1
            v_wslot = vv(lambda e, pg=pg, hb=hb, nI=nI: [
                e.tensor_copy(out=slot4(2, pg)[:, c * 1024 + hb * 128:c * 1024 + (hb + 1) * 128],
                              in_=ksl(PBbf[0], c)) for c in range(nI)][-1],
                waits=[("T", t1)], mode="drain")
            v_free_pb0 = v_wslot
        if "WQ" in dbg and ti == 0:
            sl0g = sy(lambda e: e.dma_start(out=dbg_outs["SL0"], in_=slot4(0, 0)), waits=[("V", v_wslot)], inc="SD")
            sl0g = sy(lambda e: e.dma_start(out=dbg_outs["SL1"], in_=slot4(1, 0)), inc="SD")
            kb.ev("tensor", None, waits=[("SD", sl0g)], inc=None)
        pend = None
        HTp = [HT[:, 0:512], O32[:, 0:512]]
        SGTp = [SGT[:, 0:512], SGT[:, 512:1024]]

        def emit_tail(p):
            nonlocal v_free_pb0, v_free_d
            (i0_, hq_vq, nI_, pg_, itw_) = p
            for pi, i in enumerate((i0_, i0_ + 1)):
                t1 = 0
                for c in range(nI_):
                    qo = pi * 512 + c * 128
                    t1 = tv(lambda e, c=c, qo=qo: e.transpose(ksl(PBbf[0], c), QOUT[:, qo:qo + 128], IDN[:]),
                            waits=([("V", hq_vq), ("V", v_free_pb0)] if c == 0 else []))
                qst["tcons"] = t1
                v_hqf = vv(lambda e, pi=pi, itw_=itw_: e.tensor_copy(out=HQF[:, pi * 512:pi * 512 + itw_],
                                                                     in_=PBbf[0][:, 0:itw_]),
                           waits=[("T", t1)], mode="inst")
                v_free_pb0 = v_hqf
                td = 0
                for half in range(2):
                    for c in range(nI_):
                        td = tv(lambda e, pg_=pg_, pi=pi, half=half, c=c: e.matmul(
                            PB[3 + half][:], HQF[:, pi * 512 + c * 128:pi * 512 + (c + 1) * 128],
                            slot4(2, pg_)[:, c * 1024 + half * 512:c * 1024 + half * 512 + 512],
                            start=(c == 0), stop=(c == nI_ - 1)),
                            waits=([("V", v_hqf), ("V", v_free_d)] if (half == 0 and c == 0) else []))
                if ti == 0:
                    v_free_d = vv(lambda e, i=i: [e.tensor_copy(out=ksl(XN, i, D)[:, 0:512], in_=PB[3][:]),
                                                  e.tensor_copy(out=ksl(XN, i, D)[:, 512:1024], in_=PB[4][:])][-1],
                                  waits=[("T", td)] + ([("SP", st_spill_s), ("SD", sd_spill_s)] if i == 0 else []),
                                  mode="inst")
                else:
                    v_free_d = vv(lambda e, i=i: [e.tensor_tensor(out=ksl(XN, i, D)[:, 0:512],
                                                                  in0=ksl(XN, i, D)[:, 0:512], in1=PB[3][:],
                                                                  op=ALU.add),
                                                  e.tensor_tensor(out=ksl(XN, i, D)[:, 512:1024],
                                                                  in0=ksl(XN, i, D)[:, 512:1024], in1=PB[4][:],
                                                                  op=ALU.add)][-1],
                                  waits=[("T", td)], mode="inst")
                if ti == len(IT3) - 1:
                    v_gacc[i] = v_free_d

        v_free_gp = [0, 0]
        for i0 in range(0, NCH, 2):
            v_hh = [0, 0]
            for i in (i0, i0 + 1):
                par = i % 2
                gb, ub = (1, 2) if par == 0 else (5, 6)
                tg = 0
                for k in range(NK):
                    tg = tv(lambda e, pg=pg, i=i, k=k, itw=itw, gb=gb: e.matmul(
                        PB[gb][:, 0:itw], H2F[:, i * D + k * 128:i * D + (k + 1) * 128],
                        slot4(0, pg)[:, k * itw:(k + 1) * itw], start=(k == 0), stop=(k == NK - 1)),
                        waits=([("V", v_wslot), ("V", v_xqf[i]), ("V", v_free_gp[par])] if k == 0 else []))
                tu = 0
                for k in range(NK):
                    tu = tv(lambda e, pg=pg, i=i, k=k, itw=itw, ub=ub: e.matmul(
                        PB[ub][:, 0:itw], H2F[:, i * D + k * 128:i * D + (k + 1) * 128],
                        slot4(1, pg)[:, k * itw:(k + 1) * itw], start=(k == 0), stop=(k == NK - 1)))
                if i == i0 and pend is not None:
                    emit_tail(pend)
                    pend = None
                s_sg = sc(lambda e, itw=itw, gb=gb, par=par: e.activation(out=SGTp[par][:, 0:itw],
                                                                         in_=PB[gb][:, 0:itw], func=ACT.Silu),
                          waits=[("T", tu), ("V", v_free_gp[par])], mode="drain")
                v_h = vv(lambda e, itw=itw, ub=ub, par=par: e.tensor_tensor(
                    out=HTp[par][:, 0:itw], in0=SGTp[par][:, 0:itw], in1=PB[ub][:, 0:itw],
                    op=ALU.mult), waits=[("S", s_sg)], mode="inst")
                v_free_gp[par] = v_h
                v_hh[par] = v_h
            units = [dict(sb=HTp[0][:, 0:itw], vwait=v_hh[0], w=itw, slot=0),
                     dict(sb=HTp[1][:, 0:itw], vwait=v_hh[1], w=itw, slot=1)]
            hq_vq = emit_quant_units(units, to_sbuf=True)
            pend = (i0, hq_vq, nI, pg, itw)
        if pend is not None:
            emit_tail(pend)
            pend = None
        t_tile4_done[pg] = kb.now("T")

    s_g = [0] * NCH
    for i in range(NCH):
        s_g[i] = sc(lambda e, i=i: e.activation(out=ksl(XN, i, D), in_=ksl(XN, i, D), func=ACT.Silu),
                    waits=[("V", v_gacc[i])], mode="drain")
    if "g" in dbg:
        sy(lambda e: e.dma_start(out=dbg_outs["g"].rearrange("(c p) n -> p c n", p=128),
                                 in_=XN[:].rearrange("p (c n) -> p c n", c=NCH)),
           waits=[("S", s_g[NCH - 1])], inc="SD")

    WAf = WA[:].bitcast(F32)
    bufX = [WAf[:, 0:1024], WAf[:, 1024:2048]]
    bufS = [WAf[:, 2048:3072], WAf[:, 3072:4096]]
    bufT = [WAf[:, 4096:5120], WAf[:, 5120:6144]]
    t_all = kb.now("T")
    v_out = [0] * NCH
    for i in range(NCH):
        p = i % 2
        w_pre = ([("V", v_out[i - 2])] if i >= 2 else [("T", t_all), ("V", qst["vqf"])])
        ldx = sy(lambda e, i=i, p=p: e.dma_start(out=bufX[p], in_=xo[i * 128:(i + 1) * 128, :]),
                 waits=w_pre, inc="LX%d" % p)
        lds = sy(lambda e, i=i, p=p: e.dma_start(out=bufS[p], in_=ss_d[i * 128:(i + 1) * 128, :]),
                 inc="LZ%d" % p)
        vv(lambda e, i=i, p=p: e.tensor_tensor(out=bufT[p], in0=bufS[p], in1=ksl(XN, i, D), op=ALU.mult),
           waits=[("LZ%d" % p, lds), ("S", s_g[i])])
        v_out[i] = vv(lambda e, i=i, p=p: e.tensor_tensor(out=bufT[p], in0=bufT[p], in1=bufX[p], op=ALU.add),
                      waits=[("LX%d" % p, ldx)], mode="drain")
        sy(lambda e, i=i, p=p: e.dma_start(out=out_d[i * 128:(i + 1) * 128, :], in_=bufT[p]),
           waits=[("V", v_out[i])], inc="ST")
    st_total = kb.now("ST")

    with (nc.Block() as block,):
        def emit_for(name):
            def body(eng):
                waited = {}
                for (waits, fn, inc, mode) in kb.evs[name]:
                    for (s, v) in waits:
                        if waited.get(s, 0) < v:
                            eng.wait_ge(sems[s], v)
                            waited[s] = v
                    inst = fn(eng) if fn is not None else None
                    if inc is not None:
                        if mode == "dma":
                            inst.then_inc(sems[inc], 16)
                        elif mode == "inst":
                            inst.then_inc(sems[inc], 1)
                        elif mode == "drain":
                            eng.drain().then_inc(sems[inc], 1)
                        elif mode == "nop":
                            eng.nop().then_inc(sems[inc], 1)
                if name == "sync":
                    eng.wait_ge(sems["ST"], st_total)
                    for s2 in ("SP", "SQ", "SD", "LDX", "LDP0", "LDP1", "LC", "LQ0", "LQ1",
                               "LW4", "LH", "LX0", "LX1", "LZ0", "LZ1", "SQX"):
                        if kb.now(s2):
                            eng.wait_ge(sems[s2], kb.now(s2))
            return body

        block.sync(emit_for("sync"))
        block.gpsimd(emit_for("gpsimd"))
        block.vector(emit_for("vector"))
        block.scalar(emit_for("scalar"))
        block.tensor(emit_for("tensor"))

    st_ctx.close()
    return nc, dbg_outs


_NC_CACHE = {}


def _get_program(dbg=()):
    key = tuple(sorted(dbg))
    if key not in _NC_CACHE:
        _NC_CACHE[key] = build_program(dbg)
    return _NC_CACHE[key]


def make_in_maps(x, wq, wk, wv, wo, anw, onw, mnw, wg, wu, wd, fgw, fuw, fdw):
    x = np.ascontiguousarray(np.asarray(x, dtype=np.float32))
    tri = np.triu(np.ones((128, 128), dtype=np.float32))
    idn = np.eye(128, dtype=ml_dtypes.bfloat16)
    anw_b = np.ascontiguousarray(np.broadcast_to(np.asarray(anw, np.float32), (128, D)))
    mnw_b = np.ascontiguousarray(np.broadcast_to(np.asarray(mnw, np.float32), (128, D)))
    onw_b = np.ascontiguousarray(np.broadcast_to(np.asarray(onw, np.float32), (128, 128)))
    shared = dict(
        wq=np.ascontiguousarray(wq, np.float32), wk=np.ascontiguousarray(wk, np.float32),
        wv=np.ascontiguousarray(wv, np.float32), wo=np.ascontiguousarray(wo, np.float32),
        wg=np.ascontiguousarray(wg, np.float32), wu=np.ascontiguousarray(wu, np.float32),
        wd=np.ascontiguousarray(wd, np.float32),
        fg=np.ascontiguousarray(fgw, np.float32), fu=np.ascontiguousarray(fuw, np.float32),
        fd=np.ascontiguousarray(fdw, np.float32),
        anw_b=anw_b, mnw_b=mnw_b, onw_b=onw_b, triu=tri, ident=idn,
    )
    zeros = np.zeros((R, D), dtype=np.float32)
    ones_m = np.ones((128, 1), dtype=np.float32)
    zeros_m = np.zeros((128, 1), dtype=np.float32)
    in_maps = []
    for c in range(8):
        b, half = c // 2, c % 2
        im = dict(shared)
        im["xo"] = np.ascontiguousarray(x[b, half * R:(half + 1) * R, :])
        im["xp"] = np.ascontiguousarray(x[b, 0:R, :]) if half == 1 else zeros
        im["pmask"] = ones_m if half == 1 else zeros_m
        in_maps.append(im)
    return in_maps


def kernel(L0_attn_hidden_states, L0_attn_attn_norm_w, L0_attn_Wq, L0_attn_Wk, L0_attn_Wv,
           L0_attn_Wo, L0_attn_attn_out_norm_w, L0_attn_mlp_norm_w, L0_attn_Wgate, L0_attn_Wup,
           L0_attn_Wdown, L0_ffn_w_gate, L0_ffn_w_up, L0_ffn_w_down):
    in_maps = make_in_maps(L0_attn_hidden_states, L0_attn_Wq, L0_attn_Wk, L0_attn_Wv, L0_attn_Wo,
                           L0_attn_attn_norm_w, L0_attn_attn_out_norm_w, L0_attn_mlp_norm_w,
                           L0_attn_Wgate, L0_attn_Wup, L0_attn_Wdown,
                           L0_ffn_w_gate, L0_ffn_w_up, L0_ffn_w_down)
    nc, _ = _get_program()
    res = run_bass_kernel_spmd(nc, in_maps, core_ids=list(range(8)))
    out = np.empty((4, 4096, D), dtype=np.float32)
    for c, r in enumerate(res.results):
        b, half = c // 2, c % 2
        out[b, half * R:(half + 1) * R, :] = r["out"]
    return out


# version 16 (final)
# speedup vs baseline: 1.0203x; 1.0176x over previous
import numpy as np
import ml_dtypes
import concourse.bass as bass
import concourse.mybir as mybir
from concourse.bass_utils import run_bass_kernel_spmd
from contextlib import ExitStack

F32 = mybir.dt.float32
BF16 = mybir.dt.bfloat16
AX = mybir.AxisListType
ALU = mybir.AluOpType
ACT = mybir.ActivationFunctionType

R = 2048
NCH = 16
D = 1024
NK = 8
H = 8
I2 = 4096
I3 = 2816
EPS = 1e-5

ENGINES = ("sync", "gpsimd", "vector", "scalar", "tensor")


class KB:
    def __init__(self):
        self.evs = {e: [] for e in ENGINES}
        self.cnt = {}

    def ev(self, eng, fn, waits=(), inc=None, mode="inst"):
        w = [(s, v) for (s, v) in waits if v and v > 0]
        self.evs[eng].append((w, fn, inc, mode))
        if inc is None:
            return None
        self.cnt[inc] = self.cnt.get(inc, 0) + (16 if mode == "dma" else 1)
        return self.cnt[inc]

    def now(self, sem):
        return self.cnt.get(sem, 0)


def build_program(dbg=()):
    nc = bass.Bass("TRN2", target_bir_lowering=False)
    kb = KB()

    xo = nc.dram_tensor("xo", [R, D], F32, kind="ExternalInput").ap()
    xp = nc.dram_tensor("xp", [R, D], F32, kind="ExternalInput").ap()
    wq_d = nc.dram_tensor("wq", [D, D], F32, kind="ExternalInput").ap()
    wk_d = nc.dram_tensor("wk", [D, D], F32, kind="ExternalInput").ap()
    wv_d = nc.dram_tensor("wv", [D, D], F32, kind="ExternalInput").ap()
    wo_d = nc.dram_tensor("wo", [D, D], F32, kind="ExternalInput").ap()
    wg_d = nc.dram_tensor("wg", [D, I2], F32, kind="ExternalInput").ap()
    wu_d = nc.dram_tensor("wu", [D, I2], F32, kind="ExternalInput").ap()
    wd_d = nc.dram_tensor("wd", [I2, D], F32, kind="ExternalInput").ap()
    fg_d = nc.dram_tensor("fg", [I3, D], F32, kind="ExternalInput").ap()
    fu_d = nc.dram_tensor("fu", [I3, D], F32, kind="ExternalInput").ap()
    fd_d = nc.dram_tensor("fd", [D, I3], F32, kind="ExternalInput").ap()
    anw_d = nc.dram_tensor("anw_b", [128, D], F32, kind="ExternalInput").ap()
    mnw_d = nc.dram_tensor("mnw_b", [128, D], F32, kind="ExternalInput").ap()
    onw_d = nc.dram_tensor("onw_b", [128, 128], F32, kind="ExternalInput").ap()
    tri_d = nc.dram_tensor("triu", [128, 128], F32, kind="ExternalInput").ap()
    idn_d = nc.dram_tensor("ident", [128, 128], BF16, kind="ExternalInput").ap()
    pmask_d = nc.dram_tensor("pmask", [128, 1], F32, kind="ExternalInput").ap()
    out_d = nc.dram_tensor("out", [R, D], F32, kind="ExternalOutput").ap()
    xns_d = nc.dram_tensor("xn_scr", [R, D], F32).ap()
    ss_d = nc.dram_tensor("s_scr", [R, D], F32).ap()
    xqs_d = nc.dram_tensor("xq_scr", [R, D], BF16).ap()
    fgq_d = nc.dram_tensor("fgq_scr", [I3, D], BF16).ap()
    fuq_d = nc.dram_tensor("fuq_scr", [I3, D], BF16).ap()
    fdq_d = nc.dram_tensor("fdq_scr", [D, I3], BF16).ap()
    dbg_outs = {}
    if "WQ" in dbg:
        dbg_outs["WQ"] = nc.dram_tensor("dbg_WQ", [I3, D], BF16, kind="ExternalOutput").ap()
        dbg_outs["WD"] = nc.dram_tensor("dbg_WD", [D, I3], BF16, kind="ExternalOutput").ap()
        dbg_outs["XQ"] = nc.dram_tensor("dbg_XQ", [R, D], BF16, kind="ExternalOutput").ap()
        dbg_outs["H0"] = nc.dram_tensor("dbg_H0", [128, 512], F32, kind="ExternalOutput").ap()
        dbg_outs["HQ0"] = nc.dram_tensor("dbg_HQ0", [128, 512], BF16, kind="ExternalOutput").ap()
        dbg_outs["G0"] = nc.dram_tensor("dbg_G0", [128, D], F32, kind="ExternalOutput").ap()
        dbg_outs["SL0"] = nc.dram_tensor("dbg_SL0", [128, 4096], BF16, kind="ExternalOutput").ap()
        dbg_outs["SL1"] = nc.dram_tensor("dbg_SL1", [128, 4096], BF16, kind="ExternalOutput").ap()
        dbg_outs["FU"] = nc.dram_tensor("dbg_FU", [I3, D], BF16, kind="ExternalOutput").ap()
        dbg_outs["WS3"] = nc.dram_tensor("dbg_WS3", [128, D], BF16, kind="ExternalOutput").ap()
        dbg_outs["GP0"] = nc.dram_tensor("dbg_GP0", [128, 512], F32, kind="ExternalOutput").ap()
        dbg_outs["UP0"] = nc.dram_tensor("dbg_UP0", [128, 512], F32, kind="ExternalOutput").ap()
    for name in [n for n in dbg if n != "WQ"]:
        shp = {"xn": [R, D], "x2": [R, D], "s": [R, D], "g": [R, D], "S": [128, H * 128],
               "PFX": [128, 4 * D], "MIX": [128, 6 * D], "P0": [128, 4 * D], "XP0": [128, 2 * D]}.get(name)
        dbg_outs[name] = nc.dram_tensor("dbg_" + name, shp, F32, kind="ExternalOutput").ap()

    st_ctx = ExitStack()
    en = st_ctx.enter_context

    XN = en(nc.sbuf_tensor("XN", [128, NCH * D], F32))
    H2F = en(nc.sbuf_tensor("H2F", [128, NCH * D], BF16))
    ONF = H2F[:, 0:D]
    QF = H2F[:, D:2 * D]
    KF = H2F[:, 2 * D:3 * D]
    H1F = H2F[:, 3 * D:4 * D]
    WA = en(nc.sbuf_tensor("WA", [128, 4 * NK * D], BF16))
    XP = en(nc.sbuf_tensor("XP", [128, 2 * D], F32))
    H1T = en(nc.sbuf_tensor("H1T", [128, D], BF16))
    QT = en(nc.sbuf_tensor("QT", [128, D], BF16))
    KT = en(nc.sbuf_tensor("KT", [128, D], BF16))
    VT = en(nc.sbuf_tensor("VT", [128, D], BF16))
    ATN = en(nc.sbuf_tensor("ATN", [128, 2 * 128], BF16))
    O32 = en(nc.sbuf_tensor("O32", [128, D], F32))
    S32 = en(nc.sbuf_tensor("S32", [128, H * 128], F32))
    SBF = en(nc.sbuf_tensor("SBF", [128, H * 128], BF16))
    WB1 = en(nc.sbuf_tensor("WB1", [128, D], F32))
    ONW = en(nc.sbuf_tensor("ONW", [128, 128], F32))
    TRI = en(nc.sbuf_tensor("TRI", [128, 128], F32))
    IDN = en(nc.sbuf_tensor("IDN", [128, 128], BF16))
    PAN = en(nc.sbuf_tensor("PAN", [128, 192], F32))
    PMSK = en(nc.sbuf_tensor("PMSK", [128, 1], F32))
    QSTG = XP[:, 1024:2048]
    QZ = en(nc.sbuf_tensor("QZ", [128, 1024], F32))
    QSG = en(nc.sbuf_tensor("QSG", [128, 1024], F32))
    QCB = en(nc.sbuf_tensor("QCB", [128, 1024], F32))
    QOUT = en(nc.sbuf_tensor("QOUT", [128, 1024], BF16))
    QPAN = en(nc.sbuf_tensor("QPAN", [128, 192], F32))
    HT = O32[:, 512:1024]
    JNK = XP[:, 0:512]
    SGT = KT[:]
    HQF = VT[:]
    WSTG = QT[:]

    PB = [en(nc.psum_tensor(f"PB{i}", [128, 512], F32)) for i in range(8)]
    PBbf = [PB[i].ap().bitcast(BF16) for i in range(8)]

    sems = {}
    for s in ("LD", "LW", "T", "V", "S", "ST", "SP", "SQ", "SD",
              "LDX", "LDP0", "LDP1", "LC", "W3A", "W3B", "LQ0", "LQ1", "LW4", "LH",
              "LX0", "LX1", "LZ0", "LZ1", "SQX"):
        sems[s] = en(nc.semaphore(s))

    def sy(fn, waits=(), inc="LD", mode="dma"):
        return kb.ev("sync", fn, waits, inc, mode)

    def gp(fn, waits=(), inc="LW", mode="dma"):
        return kb.ev("gpsimd", fn, waits, inc, mode)

    def tv(fn, waits=(), inc="T", mode="inst"):
        return kb.ev("tensor", fn, waits, inc, mode)

    def vv(fn, waits=(), inc="V", mode="inst"):
        return kb.ev("vector", fn, waits, inc, mode)

    def sc(fn, waits=(), inc="S", mode="inst"):
        return kb.ev("scalar", fn, waits, inc, mode)

    def ksl(buf, k, w=128):
        return buf[:, k * w:(k + 1) * w]

    def wsrc(d):
        return d.rearrange("(k p) n -> p k n", p=128)

    sy(lambda e: e.dma_start(out=TRI[:], in_=tri_d), inc="LC")
    sy(lambda e: e.dma_start(out=IDN[:], in_=idn_d), inc="LC")
    sy(lambda e: e.dma_start(out=ONW[:], in_=onw_d), inc="LC")
    ld_const = sy(lambda e: e.dma_start(out=WB1[:], in_=anw_d), inc="LC")
    ld_pmsk = sy(lambda e: e.dma_start(out=PMSK[:], in_=pmask_d), inc="LC")
    ld0 = ld_const
    gp(lambda e: e.dma_start(out=WA[:, 0 * NK * D:1 * NK * D].rearrange("p (k n) -> p k n", k=NK), in_=wsrc(wq_d)))
    gp(lambda e: e.dma_start(out=WA[:, 1 * NK * D:2 * NK * D].rearrange("p (k n) -> p k n", k=NK), in_=wsrc(wk_d)))
    lw_v = gp(lambda e: e.dma_start(out=WA[:, 2 * NK * D:3 * NK * D].rearrange("p (k n) -> p k n", k=NK), in_=wsrc(wv_d)))
    lw_o = gp(lambda e: e.dma_start(out=WA[:, 3 * NK * D:4 * NK * D].rearrange("p (k n) -> p k n", k=NK), in_=wsrc(wo_d)))
    WQs = WA[:, 0 * NK * D:1 * NK * D]
    WKs = WA[:, 1 * NK * D:2 * NK * D]
    WVs = WA[:, 2 * NK * D:3 * NK * D]
    WOs = WA[:, 3 * NK * D:4 * NK * D]

    v_setup = vv(lambda e: [e.memset(PAN[:], 0.0), e.memset(QPAN[:], 0.0), e.memset(S32[:], 0.0)][-1], mode="drain")
    sc(lambda e: e.preload_act_table(ACT.Sqrt), inc=None)

    qst = {"ssg": 0, "st": 0, "vqf": 0, "tcons": 0, "stSQ": 0, "stSQX": 0}

    def emit_quant_units(units, to_sbuf=False, store_sem="SQ"):
        n = len(units)
        lds = [0] * n
        ldsem = [None] * n
        for ui, u in enumerate(units):
            if u.get("src") is not None:
                ldsem[ui] = "LQ%d" % u["slot"]
                lds[ui] = sy(lambda e, u=u: e.dma_start(out=QSTG[:, u["slot"] * 512:u["slot"] * 512 + u["w"]],
                                                        in_=u["src"]),
                             waits=[("S", qst["ssg"]), ("V", qst.get("ldv", 0))], inc=ldsem[ui])
        ssg = qst["ssg"]
        for ui, u in enumerate(units):
            j, w = u["slot"], u["w"]
            src_sb = QSTG[:, j * 512:j * 512 + w] if u.get("src") is not None else u["sb"]
            waits = ([(ldsem[ui], lds[ui])] if u.get("src") is not None else [("V", u.get("vwait", 0))])
            waits = waits + [("V", qst["vqf"])]
            sc(lambda e, src_sb=src_sb, j=j, w=w: e.activation(out=QZ[:, j * 512:j * 512 + w], in_=src_sb, func=ACT.Abs),
               waits=waits, inc=None)
            ssg = sc(lambda e, src_sb=src_sb, j=j, w=w: e.sign(out=QSG[:, j * 512:j * 512 + w], in_=src_sb),
                     mode="drain")
        qst["ssg"] = ssg
        for ui, u in enumerate(units):
            j, w = u["slot"], u["w"]
            vv(lambda e, j=j, w=w: e.tensor_reduce(
                out=QPAN[:, j * 32:j * 32 + w // 16],
                in_=QZ[:, j * 512:j * 512 + w].rearrange("p (b k) -> p b k", k=16),
                axis=AX.X, op=ALU.max, apply_absolute_value=True),
               waits=[("S", ssg)], mode="drain" if ui == n - 1 else "inst")
        for ui, u in enumerate(units):
            j, w = u["slot"], u["w"]
            vv(lambda e, j=j, w=w: e.tensor_scalar(
                out=QPAN[:, 64 + j * 32:64 + j * 32 + w // 16], in0=QPAN[:, j * 32:j * 32 + w // 16],
                scalar1=1.0 / 12, scalar2=5e-13, op0=ALU.mult, op1=ALU.max),
               mode="drain" if ui == n - 1 else "inst")
        for ui, u in enumerate(units):
            j, w = u["slot"], u["w"]
            vv(lambda e, j=j, w=w: e.reciprocal(
                out=QPAN[:, 128 + j * 32:128 + j * 32 + w // 16], in_=QPAN[:, 64 + j * 32:64 + j * 32 + w // 16]),
               mode="drain" if ui == n - 1 else "inst")
        for u in units:
            j, w = u["slot"], u["w"]
            invb = QPAN[:, 128 + j * 32:128 + j * 32 + w // 16].unsqueeze(2).broadcast_to([128, w // 16, 16])
            vv(lambda e, j=j, w=w, invb=invb: e.scalar_tensor_tensor(
                out=QZ[:, j * 512:j * 512 + w].rearrange("p (b k) -> p b k", k=16),
                in0=QZ[:, j * 512:j * 512 + w].rearrange("p (b k) -> p b k", k=16),
                scalar=0.5, in1=invb, op0=ALU.mult, op1=ALU.mult))

        def chain(m, first=False):
            for u in units:
                j, w = u["slot"], u["w"]
                zj = QZ[:, j * 512:j * 512 + w]
                cj = QCB[:, j * 512:j * 512 + w]
                if first:
                    vv(lambda e, zj=zj, cj=cj, m=m: e.tensor_scalar(out=cj, in0=zj, scalar1=m, scalar2=None,
                                                                    op0=ALU.is_gt))
                else:
                    vv(lambda e, zj=zj, cj=cj, m=m: e.scalar_tensor_tensor(out=cj, in0=zj, scalar=m, in1=cj,
                                                                           op0=ALU.is_gt, op1=ALU.add))

        chain(2.5, True)
        chain(3.5)
        chain(5.0)
        chain(5.0)
        for u in units:
            j, w = u["slot"], u["w"]
            cj = QCB[:, j * 512:j * 512 + w]
            vv(lambda e, cj=cj: e.tensor_scalar(out=cj, in0=cj, scalar1=2.0, scalar2=None, op0=ALU.mult))
        chain(0.25)
        chain(0.75)
        chain(1.25)
        chain(1.75)
        for u in units:
            j, w = u["slot"], u["w"]
            sclb = QPAN[:, 64 + j * 32:64 + j * 32 + w // 16].unsqueeze(2).broadcast_to([128, w // 16, 16])
            vv(lambda e, j=j, w=w, sclb=sclb: e.tensor_tensor(
                out=QSG[:, j * 512:j * 512 + w].rearrange("p (b k) -> p b k", k=16),
                in0=QSG[:, j * 512:j * 512 + w].rearrange("p (b k) -> p b k", k=16), in1=sclb, op=ALU.mult))
        vqf = qst["vqf"]
        for ui, u in enumerate(units):
            j, w = u["slot"], u["w"]
            waits = ([(store_sem, qst["st" + store_sem])] if not to_sbuf else [("T", qst["tcons"])])
            vqf = vv(lambda e, j=j, w=w: e.tensor_tensor(
                out=QOUT[:, j * 512:j * 512 + w], in0=QCB[:, j * 512:j * 512 + w],
                in1=QSG[:, j * 512:j * 512 + w], op=ALU.mult),
                waits=waits if ui == 0 else [], mode="drain" if ui == n - 1 else "inst")
        qst["vqf"] = vqf
        if not to_sbuf:
            stq = qst["st" + store_sem]
            for u in units:
                j, w = u["slot"], u["w"]
                stq = sy(lambda e, u=u, j=j, w=w: e.dma_start(out=u["dst"], in_=QOUT[:, j * 512:j * 512 + w]),
                         waits=[("V", vqf)], inc=store_sem)
            qst["st" + store_sem] = stq
        return vqf

    ld_xo_all = sy(lambda e: e.dma_start(out=XN[:].rearrange("p (c n) -> p c n", c=NCH),
                                         in_=xo.rearrange("(c p) n -> p c n", p=128)), inc="LDX")
    s_sq = 0
    for i in range(NCH):
        vv(lambda e, i=i: e.reduce_sum(out=PAN[:, i:i + 1], in_=ksl(XN, i, D), axis=AX.X),
           waits=[("LDX", ld_xo_all), ("V", v_setup)])
        s_sq = sc(lambda e, i=i: e.activation(out=O32[:], in_=ksl(XN, i, D), func=ACT.Square,
                                              accum_out=PAN[:, 16 + i:17 + i]),
                  waits=[("LDX", ld_xo_all), ("V", v_setup)], mode="drain")
    vv(lambda e: e.tensor_scalar(out=PAN[:, 32:48], in0=PAN[:, 0:16], scalar1=1.0 / D, scalar2=None, op0=ALU.mult),
       waits=[("S", s_sq)], mode="drain")
    vv(lambda e: e.tensor_scalar(out=PAN[:, 144:160], in0=PAN[:, 16:32], scalar1=1.0 / D, scalar2=EPS,
                                 op0=ALU.mult, op1=ALU.add), mode="drain")
    vv(lambda e: e.tensor_tensor(out=PAN[:, 160:176], in0=PAN[:, 32:48], in1=PAN[:, 32:48], op=ALU.mult), mode="drain")
    vv(lambda e: e.tensor_tensor(out=PAN[:, 144:160], in0=PAN[:, 144:160], in1=PAN[:, 160:176], op=ALU.subtract),
       mode="drain")
    v_tr = vv(lambda e: e.reciprocal(out=PAN[:, 160:176], in_=PAN[:, 144:160]), mode="drain")
    s_tr = sc(lambda e: e.sqrt(out=PAN[:, 48:64], in_=PAN[:, 160:176]), waits=[("V", v_tr)], mode="drain")
    v_ln = [0] * NCH
    s_rms = 0
    for i in range(NCH):
        v_ln[i] = vv(lambda e, i=i: e.tensor_scalar(out=ksl(XN, i, D), in0=ksl(XN, i, D),
                                                    scalar1=PAN[:, 32 + i:33 + i], scalar2=PAN[:, 48 + i:49 + i],
                                                    op0=ALU.subtract, op1=ALU.mult),
                     waits=[("S", s_tr)], mode="drain")
        s_rms = sc(lambda e, i=i: e.activation(out=O32[:], in_=ksl(XN, i, D), func=ACT.Square,
                                               accum_out=PAN[:, 64 + i:65 + i]),
                   waits=[("V", v_ln[i])], mode="drain")
    st_xn_spill = sy(lambda e: e.dma_start(out=xns_d.rearrange("(c p) n -> p c n", p=128),
                                           in_=XN[:].rearrange("p (c n) -> p c n", c=NCH)),
                     waits=[("V", v_ln[NCH - 1])], inc="SP")
    if "xn" in dbg:
        sy(lambda e: e.dma_start(out=dbg_outs["xn"].rearrange("(c p) n -> p c n", p=128),
                                 in_=XN[:].rearrange("p (c n) -> p c n", c=NCH)),
           waits=[("V", v_ln[NCH - 1])], inc="SP")
        st_xn_spill = kb.now("SP")
    sd_xn = kb.now("SD")
    vv(lambda e: e.tensor_scalar(out=PAN[:, 144:160], in0=PAN[:, 64:80], scalar1=1.0 / D, scalar2=EPS,
                                 op0=ALU.mult, op1=ALU.add),
       waits=[("S", s_rms)], mode="drain")
    v_tr = vv(lambda e: e.reciprocal(out=PAN[:, 160:176], in_=PAN[:, 144:160]), mode="drain")
    s_rstd1 = sc(lambda e: e.sqrt(out=PAN[:, 80:96], in_=PAN[:, 160:176]), waits=[("V", v_tr)], mode="drain")

    xq_ssg = [0] * NCH

    def emit_xq(i):
        emit_quant_units([
            dict(sb=ksl(XN, i, D)[:, 0:512], vwait=v_ln[i],
                 dst=xqs_d[i * 128:(i + 1) * 128, 0:512], w=512, slot=0),
            dict(sb=ksl(XN, i, D)[:, 512:1024], vwait=v_ln[i],
                 dst=xqs_d[i * 128:(i + 1) * 128, 512:1024], w=512, slot=1),
        ], store_sem="SQX")
        xq_ssg[i] = qst["ssg"]

    wq_units = []
    for (src_d, dst_dq, rows, cols) in ((fg_d, fgq_d, I3, D), (fu_d, fuq_d, I3, D)):
        for rb in range(rows // 128):
            for cb in range(cols // 512):
                wq_units.append(dict(src=src_d[rb * 128:(rb + 1) * 128, cb * 512:(cb + 1) * 512],
                                     dst=dst_dq[rb * 128:(rb + 1) * 128, cb * 512:(cb + 1) * 512], w=512))
    for rb in range(D // 128):
        for cb in range((I3 + 511) // 512):
            w = min(512, I3 - cb * 512)
            wq_units.append(dict(src=fd_d[rb * 128:(rb + 1) * 128, cb * 512:cb * 512 + w],
                                 dst=fdq_d[rb * 128:(rb + 1) * 128, cb * 512:cb * 512 + w], w=w))
    qi = [0]

    def emit_wq(npairs):
        for _ in range(npairs):
            if qi[0] >= len(wq_units):
                return
            batch = wq_units[qi[0]:qi[0] + 2]
            qi[0] += len(batch)
            for j, u in enumerate(batch):
                u["slot"] = j
            emit_quant_units(batch)

    t_h1tr = 0
    v_free_pb0 = 0
    v_free_kv = 0
    t_sprev = 0
    ld_xp = [0] * NCH
    ld_xp[0] = sy(lambda e: e.dma_start(out=XP[:, 0:D], in_=xp[0:128, :]), inc="LDP0")
    ld_xp[1] = sy(lambda e: e.dma_start(out=XP[:, D:2 * D], in_=xp[128:256, :]), inc="LDP1")
    for i in range(NCH):
        pg = i % 2
        xpc = XP[:, pg * D:(pg + 1) * D]
        if "XP0" in dbg and i == 0:
            xp0_gate = sy(lambda e: e.dma_start(out=dbg_outs["XP0"][:, 0:D], in_=XP[:, 0:D]),
                          waits=[("LDP0", ld_xp[0])], inc="SD")
            kb.ev("vector", None, waits=[("SD", xp0_gate)], inc=None)
        v1 = vv(lambda e, xpc=xpc: e.reduce_sum(out=PAN[:, 176:177], in_=xpc, axis=AX.X),
                waits=[("LDP%d" % (i % 2), ld_xp[i])], mode="drain")
        s1 = sc(lambda e, xpc=xpc: e.activation(out=O32[:], in_=xpc, func=ACT.Square,
                                                accum_out=PAN[:, 177:178]),
                waits=[("LDP%d" % (i % 2), ld_xp[i]), ("V", v_setup)], mode="drain")
        vv(lambda e: e.tensor_scalar(out=PAN[:, 178:180], in0=PAN[:, 176:178], scalar1=1.0 / D, scalar2=None,
                                     op0=ALU.mult), waits=[("S", s1), ("V", v1)], mode="drain")
        vv(lambda e: e.tensor_tensor(out=PAN[:, 180:181], in0=PAN[:, 178:179], in1=PAN[:, 178:179], op=ALU.mult),
           mode="drain")
        vv(lambda e: e.tensor_tensor(out=PAN[:, 181:182], in0=PAN[:, 179:180], in1=PAN[:, 180:181], op=ALU.subtract),
           mode="drain")
        vv(lambda e: e.tensor_scalar(out=PAN[:, 181:182], in0=PAN[:, 181:182], scalar1=EPS, scalar2=None, op0=ALU.add),
           mode="drain")
        v1 = vv(lambda e: e.reciprocal(out=PAN[:, 182:183], in_=PAN[:, 181:182]), mode="drain")
        s1 = sc(lambda e: e.sqrt(out=PAN[:, 183:184], in_=PAN[:, 182:183]), waits=[("V", v1)], mode="drain")
        v1 = vv(lambda e, xpc=xpc: e.tensor_scalar(out=xpc, in0=xpc, scalar1=PAN[:, 178:179], scalar2=PAN[:, 183:184],
                                                   op0=ALU.subtract, op1=ALU.mult),
                waits=[("S", s1)], mode="drain")
        s1 = sc(lambda e, xpc=xpc: e.activation(out=O32[:], in_=xpc, func=ACT.Square,
                                                accum_out=PAN[:, 185:186]), waits=[("V", v1)], mode="drain")
        v1 = vv(lambda e: e.tensor_scalar(out=PAN[:, 177:178], in0=PAN[:, 185:186], scalar1=1.0 / D, scalar2=EPS,
                                          op0=ALU.mult, op1=ALU.add), waits=[("S", s1)], mode="drain")
        v1 = vv(lambda e: e.reciprocal(out=PAN[:, 178:179], in_=PAN[:, 177:178]), mode="drain")
        s1 = sc(lambda e: e.sqrt(out=PAN[:, 179:180], in_=PAN[:, 178:179]), waits=[("V", v1)], mode="drain")
        v_h1 = vv(lambda e, xpc=xpc: e.scalar_tensor_tensor(out=H1T[:], in0=xpc, scalar=PAN[:, 179:180], in1=WB1[:],
                                                            op0=ALU.mult, op1=ALU.mult),
                  waits=[("S", s1), ("LC", ld_const), ("T", t_h1tr)], mode="drain")
        if i + 2 < NCH:
            ld_xp[i + 2] = sy(lambda e, i=i, pg=pg: e.dma_start(out=XP[:, pg * D:(pg + 1) * D],
                                                                in_=xp[(i + 2) * 128:(i + 3) * 128, :]),
                              waits=[("V", v_h1)], inc="LDP%d" % pg)
        t1 = 0
        for k in range(NK):
            t1 = tv(lambda e, k=k: e.transpose(ksl(PBbf[0], k), ksl(H1T, k), IDN[:]),
                    waits=([("V", v_h1), ("V", v_free_pb0), ("LC", ld0), ("LW", lw_o)] if k == 0 else []))
        t_h1tr = t1
        v_h1f = vv(lambda e: e.tensor_copy(out=H1F[:], in_=PBbf[0][:]), waits=[("T", t1)], mode="inst")
        v_free_pb0 = v_h1f
        tk_ = 0
        for half in range(2):
            for k in range(NK):
                tk_ = tv(lambda e, half=half, k=k: e.matmul(PB[1 + half][:], ksl(H1F, k),
                                                            WKs[:, k * D + half * 512:k * D + half * 512 + 512],
                                                            start=(k == 0), stop=(k == NK - 1)),
                         waits=([("V", v_h1f), ("V", v_free_kv)] if (half == 0 and k == 0) else []))
        for half in range(2):
            for k in range(NK):
                tk_ = tv(lambda e, half=half, k=k: e.matmul(PB[3 + half][:], ksl(H1F, k),
                                                            WVs[:, k * D + half * 512:k * D + half * 512 + 512],
                                                            start=(k == 0), stop=(k == NK - 1)))
        v_kt = vv(lambda e: [e.tensor_copy(out=KT[:, 0:512], in_=PB[1][:]),
                             e.tensor_copy(out=KT[:, 512:1024], in_=PB[2][:]),
                             e.tensor_copy(out=VT[:, 0:512], in_=PB[3][:]),
                             e.tensor_copy(out=VT[:, 512:1024], in_=PB[4][:])][-1],
                  waits=[("T", tk_)], mode="inst")
        v_free_kv = v_kt
        for h in range(H):
            bank, off = (6 + h // 4), (h % 4) * 128
            t_sprev = tv(lambda e, h=h, bank=bank, off=off: e.matmul(
                PB[bank][:, off:off + 128], ksl(KT, h), ksl(VT, h),
                start=(h % 4 == 0), stop=(h % 4 == 3), skip_group_check=True),
                waits=([("V", v_kt)] + ([("V", v_supd_p)] if i > 0 else [("V", v_setup)])) if h == 0 else [])
        v_supd_p = vv(lambda e: [e.tensor_tensor(out=S32[:, 0:512], in0=S32[:, 0:512], in1=PB[6][:], op=ALU.add),
                                 e.tensor_tensor(out=S32[:, 512:1024], in0=S32[:, 512:1024], in1=PB[7][:],
                                                 op=ALU.add)][-1],
                      waits=[("T", t_sprev)], mode="inst")
        if "P0" in dbg and i == 0:
            sy(lambda e: e.dma_start(out=dbg_outs["P0"][:, 0:D].bitcast(BF16)[:, 0:D], in_=KT[:]),
               waits=[("V", v_supd_p)], inc="SD")
            sy(lambda e: e.dma_start(out=dbg_outs["P0"][:, D:2 * D].bitcast(BF16)[:, 0:D], in_=VT[:]), inc="SD")
            sy(lambda e: e.dma_start(out=dbg_outs["P0"][:, 2 * D:3 * D], in_=S32[:]), inc="SD")
            p0_gate = sy(lambda e: e.dma_start(out=dbg_outs["P0"][:, 3 * D:4 * D].bitcast(BF16)[:, 0:D], in_=H1F[:]),
                         inc="SD")
            kb.ev("vector", None, waits=[("SD", p0_gate)], inc=None)
        if "PFX" in dbg and i == NCH - 1:
            sy(lambda e, xpc=xpc: e.dma_start(out=dbg_outs["PFX"][:, 0:D], in_=xpc),
               waits=[("V", v_kt)], inc="SD")
            sy(lambda e: e.dma_start(out=dbg_outs["PFX"][:, D:2 * D].bitcast(BF16)[:, 0:D], in_=H1T[:]), inc="SD")
            sy(lambda e: e.dma_start(out=dbg_outs["PFX"][:, 2 * D:3 * D].bitcast(BF16)[:, 0:D], in_=KT[:]), inc="SD")
            pfx_gate = sy(lambda e: e.dma_start(out=dbg_outs["PFX"][:, 3 * D:4 * D].bitcast(BF16)[:, 0:D], in_=VT[:]), inc="SD")
            kb.ev("vector", None, waits=[("SD", pfx_gate)], inc=None)
        emit_xq(i)
    v_msk = vv(lambda e: e.tensor_scalar(out=S32[:], in0=S32[:], scalar1=PMSK[:], scalar2=None, op0=ALU.mult),
               waits=[("V", v_supd_p), ("LC", ld_pmsk)], mode="drain")
    v_sbf = vv(lambda e: e.tensor_copy(out=SBF[:], in_=S32[:]), mode="inst")
    dbg_s_gate = 0
    if "S" in dbg:
        dbg_s_gate = sy(lambda e: e.dma_start(out=dbg_outs["S"], in_=S32[:]), waits=[("V", v_sbf)], inc="SD")

    qst["ldv"] = kb.now("V")
    v_x2 = [0] * NCH
    v_oev_prev = 0
    s_h2_prev = 0
    s_or_prev = 0
    v_supd_prev = 0
    s_h2_last = 0
    for i in range(NCH):
        v_h1 = vv(lambda e, i=i: e.scalar_tensor_tensor(out=H1T[:], in0=ksl(XN, i, D), scalar=PAN[:, 80 + i:81 + i],
                                                        in1=WB1[:], op0=ALU.mult, op1=ALU.mult),
                  waits=[("S", s_rstd1), ("V", v_sbf), ("T", t_h1tr), ("S", xq_ssg[i])], mode="drain")
        t1 = 0
        for k in range(NK):
            t1 = tv(lambda e, k=k: e.transpose(ksl(PBbf[0], k), ksl(H1T, k), IDN[:]),
                    waits=([("V", v_h1), ("V", v_free_pb0)] if k == 0 else []))
        t_h1tr = t1
        v_h1f = vv(lambda e: e.tensor_copy(out=H1F[:], in_=PBbf[0][:]), waits=[("T", t1)], mode="inst")
        v_free_pb0 = v_h1f

        def proj(Wsb, dst, v_ready):
            tlast = 0
            for half in range(2):
                for k in range(NK):
                    tlast = tv(lambda e, half=half, k=k, Wsb=Wsb: e.matmul(
                        PB[1 + half][:], ksl(H1F, k), Wsb[:, k * D + half * 512:k * D + half * 512 + 512],
                        start=(k == 0), stop=(k == NK - 1)),
                        waits=([("V", v_ready)] if (half == 0 and k == 0) else []))
            return tlast, vv(lambda e, dst=dst: [e.tensor_copy(out=dst[:, 0:512], in_=PB[1][:]),
                                                 e.tensor_copy(out=dst[:, 512:1024], in_=PB[2][:])][-1],
                             waits=[("T", tlast)], mode="inst")

        _, v_q = proj(WQs, QT, v_h1f)
        _, v_k = proj(WKs, KT, v_q)
        _, v_v = proj(WVs, VT, v_k)
        t1 = 0
        for k in range(NK):
            t1 = tv(lambda e, k=k: e.transpose(ksl(PBbf[0], k), ksl(QT, k), IDN[:]),
                    waits=([("V", v_q), ("V", v_free_pb0)] if k == 0 else []))
        v_qf = vv(lambda e: e.tensor_copy(out=QF[:], in_=PBbf[0][:]), waits=[("T", t1)], mode="inst")
        t1 = 0
        for k in range(NK):
            t1 = tv(lambda e, k=k: e.transpose(ksl(PBbf[0], k), ksl(KT, k), IDN[:]),
                    waits=([("V", v_k), ("V", v_qf)] if k == 0 else []))
        v_kf = vv(lambda e: e.tensor_copy(out=KF[:], in_=PBbf[0][:]), waits=[("T", t1)], mode="inst")
        v_free_pb0 = v_kf
        v_atn = {}
        t_o = [0] * H
        t_su = [0] * H
        for h in range(H):
            abank = 3 if h % 2 == 0 else 0
            t_at = tv(lambda e, h=h, abank=abank: e.matmul(PB[abank][:, 0:128], ksl(KF, h), ksl(QF, h),
                                                           start=True, stop=True),
                      waits=([("V", v_kf), ("V", v_v)] if h == 0 else
                             ([("V", v_kf)] if h == 1 else [("V", v_atn[h - 2])])))
            v_atn[h] = vv(lambda e, h=h, abank=abank: e.tensor_tensor(out=ATN[:, (h % 2) * 128:(h % 2) * 128 + 128],
                                                                      in0=PB[abank][:, 0:128], in1=TRI[:], op=ALU.mult),
                          waits=[("T", t_at)], mode="drain")
            bank, off = 4 + h // 4, (h % 4) * 128
            t_o[h] = tv(lambda e, h=h, bank=bank, off=off: [
                e.matmul(PB[bank][:, off:off + 128], ATN[:, (h % 2) * 128:(h % 2) * 128 + 128], ksl(VT, h),
                         start=(h % 4 == 0), stop=False, skip_group_check=True),
                e.matmul(PB[bank][:, off:off + 128], ksl(QF, h), ksl(SBF, h), start=False,
                         stop=(h % 4 == 3), skip_group_check=True)][-1],
                waits=[("V", v_atn[h])] + ([("V", v_oev_prev)] if h == 0 else []))
            bank2, off2 = 6 + h // 4, (h % 4) * 128
            t_su[h] = tv(lambda e, h=h, bank2=bank2, off2=off2: e.matmul(
                PB[bank2][:, off2:off2 + 128], ksl(KT, h), ksl(VT, h),
                start=(h % 4 == 0), stop=(h % 4 == 3), skip_group_check=True),
                waits=(([("V", v_sbf if i == 0 else v_supd_prev)] +
                        ([("SD", dbg_s_gate)] if i == 0 else [])) if h == 0 else []))
        v_oev = vv(lambda e: [e.tensor_copy(out=O32[:, 0:512], in_=PB[4][:]),
                              e.tensor_copy(out=O32[:, 512:1024], in_=PB[5][:])][-1],
                   waits=[("T", t_o[H - 1]), ("S", s_h2_prev)], mode="inst")
        v_oev_prev = v_oev
        s_oss = 0
        for h in range(H):
            s_oss = sc(lambda e, h=h: e.activation(out=JNK[:, 0:128], in_=ksl(O32, h), func=ACT.Square,
                                                   accum_out=PAN[:, 128 + h:129 + h]),
                       waits=([("V", v_oev)] if h == 0 else []), mode="drain" if h == H - 1 else "inst")
        vv(lambda e: e.tensor_scalar(out=PAN[:, 144:152], in0=PAN[:, 128:136], scalar1=1.0 / 128, scalar2=EPS,
                                     op0=ALU.mult, op1=ALU.add), waits=[("S", s_oss)], mode="drain")
        v1 = vv(lambda e: e.reciprocal(out=PAN[:, 152:160], in_=PAN[:, 144:152]), mode="drain")
        s_or = sc(lambda e: e.sqrt(out=PAN[:, 136:144], in_=PAN[:, 152:160]), waits=[("V", v1)], mode="drain")
        s_or_prev = s_or
        v_on = 0
        for h in range(H):
            v_on = vv(lambda e, h=h: e.scalar_tensor_tensor(out=ksl(ONF, h), in0=ksl(O32, h),
                                                            scalar=PAN[:, 136 + h:137 + h], in1=ONW[:],
                                                            op0=ALU.mult, op1=ALU.mult),
                      waits=([("S", s_or)] if h == 0 else []), mode="drain" if h == H - 1 else "inst")
        t1 = 0
        for h in range(H):
            t1 = tv(lambda e, h=h: e.transpose(ksl(PBbf[0], h), ksl(ONF, h), IDN[:]),
                    waits=([("V", v_on), ("V", v_atn.get(7, 0))] if h == 0 else []))
        v_onf = vv(lambda e: e.tensor_copy(out=H1F[:], in_=PBbf[0][:]), waits=[("T", t1)], mode="inst")
        v_free_pb0 = v_onf
        t_wo = 0
        for half in range(2):
            for h in range(H):
                t_wo = tv(lambda e, half=half, h=h: e.matmul(
                    PB[1 + half][:], ksl(H1F, h), WOs[:, h * D + half * 512:h * D + half * 512 + 512],
                    start=(h == 0), stop=(h == H - 1)),
                    waits=([("V", v_onf), ("LW", lw_o)] if (half == 0 and h == 0) else []))
        v_x2[i] = vv(lambda e, i=i: [e.tensor_tensor(out=ksl(XN, i, D)[:, 0:512], in0=ksl(XN, i, D)[:, 0:512],
                                                     in1=PB[1][:], op=ALU.add),
                                     e.tensor_tensor(out=ksl(XN, i, D)[:, 512:1024], in0=ksl(XN, i, D)[:, 512:1024],
                                                     in1=PB[2][:], op=ALU.add)][-1],
                     waits=[("T", t_wo)] + ([("SP", st_xn_spill)] if i == 0 else []), mode="inst")
        vv(lambda e: [e.tensor_tensor(out=S32[:, 0:512], in0=S32[:, 0:512], in1=PB[6][:], op=ALU.add),
                      e.tensor_tensor(out=S32[:, 512:1024], in0=S32[:, 512:1024], in1=PB[7][:], op=ALU.add)][-1],
           waits=[("T", t_su[H - 1])])
        v_supd_prev = vv(lambda e: e.tensor_copy(out=SBF[:], in_=S32[:]), mode="inst")
        s_h2_last = sc(lambda e, i=i: e.activation(out=O32[:], in_=ksl(XN, i, D), func=ACT.Square,
                                                   accum_out=PAN[:, 96 + i:97 + i]),
                       waits=[("V", v_x2[i])], mode="drain")
        s_h2_prev = s_h2_last
        if "MIX" in dbg and i == 0:
            sy(lambda e: e.dma_start(out=dbg_outs["MIX"][:, 0:D].bitcast(BF16)[:, 0:D], in_=QT[:]),
               waits=[("V", v_on)], inc="SD")
            sy(lambda e: e.dma_start(out=dbg_outs["MIX"][:, D:2 * D].bitcast(BF16)[:, 0:D], in_=KT[:]), inc="SD")
            sy(lambda e: e.dma_start(out=dbg_outs["MIX"][:, 2 * D:3 * D].bitcast(BF16)[:, 0:D], in_=VT[:]), inc="SD")
            sy(lambda e: e.dma_start(out=dbg_outs["MIX"][:, 3 * D:4 * D], in_=O32[:]), inc="SD")
            sy(lambda e: e.dma_start(out=dbg_outs["MIX"][:, 4 * D:5 * D].bitcast(BF16)[:, 0:D], in_=ONF[:]), inc="SD")
            dbg_gate = sy(lambda e: e.dma_start(out=dbg_outs["MIX"][:, 5 * D:5 * D + 192], in_=PAN[:]), inc="SD")
            kb.ev("vector", None, waits=[("SD", dbg_gate)], inc=None)

    dbg_x2_gate = 0
    if "x2" in dbg:
        dbg_x2_gate = sy(lambda e: e.dma_start(out=dbg_outs["x2"].rearrange("(c p) n -> p c n", p=128),
                                               in_=XN[:].rearrange("p (c n) -> p c n", c=NCH)),
                         waits=[("V", v_x2[NCH - 1])], inc="SD")

    t_after_s2_main = kb.now("T")
    ld_mnw = sy(lambda e: e.dma_start(out=WB1[:], in_=mnw_d), waits=[("V", kb.now("V"))], inc="LC")
    vv(lambda e: e.tensor_scalar(out=PAN[:, 144:160], in0=PAN[:, 96:112], scalar1=1.0 / D, scalar2=EPS,
                                 op0=ALU.mult, op1=ALU.add), waits=[("S", s_h2_last)], mode="drain")
    v1 = vv(lambda e: e.reciprocal(out=PAN[:, 160:176], in_=PAN[:, 144:160]), mode="drain")
    s_rstd2 = sc(lambda e: e.sqrt(out=PAN[:, 112:128], in_=PAN[:, 160:176]), waits=[("V", v1)], mode="drain")
    v_h2f = [0] * NCH
    for i in range(NCH):
        v_h2 = vv(lambda e, i=i: e.scalar_tensor_tensor(out=H1T[:], in0=ksl(XN, i, D), scalar=PAN[:, 112 + i:113 + i],
                                                        in1=WB1[:], op0=ALU.mult, op1=ALU.mult),
                  waits=[("S", s_rstd2), ("LC", ld_mnw), ("T", t_h1tr)], mode="drain")
        t1 = 0
        for k in range(NK):
            t1 = tv(lambda e, k=k: e.transpose(ksl(PBbf[0], k), ksl(H1T, k), IDN[:]),
                    waits=([("V", v_h2), ("V", v_free_pb0)] if k == 0 else []))
        t_h1tr = t1
        v_h2f[i] = vv(lambda e, i=i: e.tensor_copy(out=ksl(H2F, i, D), in_=PBbf[0][:]), waits=[("T", t1)], mode="inst")
        v_free_pb0 = v_h2f[i]
        emit_wq(1)

    sc(lambda e: e.preload_act_table(ACT.Silu), waits=[("S", kb.now("S"))], inc=None)
    NT2 = I2 // 512

    def slot(s, pg):
        return WA[:, (pg * 3 + s) * 4096:(pg * 3 + s + 1) * 4096]

    lw_t = [0] * NT2
    t_after_s2 = kb.now("T")
    t_tile_done = {0: t_after_s2, 1: t_after_s2}
    v_free_d = 0
    v_s_done = 0
    v_free_gp3 = [0, 0]
    t_a2cons = [0, 0]
    pend3 = [None]
    for it in range(NT2):
        pg = it % 2
        w_wait = [("T", t_tile_done[pg])]
        wsem = "W3A" if pg == 0 else "W3B"
        gp(lambda e, it=it, pg=pg: e.dma_start(
            out=slot(0, pg).rearrange("p (k n) -> p k n", k=NK),
            in_=wsrc(wg_d)[:, :, it * 512:(it + 1) * 512]), waits=w_wait, inc=wsem)
        gp(lambda e, it=it, pg=pg: e.dma_start(
            out=slot(1, pg).rearrange("p (k n) -> p k n", k=NK),
            in_=wsrc(wu_d)[:, :, it * 512:(it + 1) * 512]), inc=wsem)
        lw_t[it] = gp(lambda e, it=it, pg=pg: e.dma_start(
            out=slot(2, pg).rearrange("p (k n) -> p k n", k=4),
            in_=wd_d.rearrange("(k p) n -> p k n", p=128)[:, it * 4:(it + 1) * 4, :]), inc=wsem)
        lw_sem = wsem

        def emit_tail3(p):
            nonlocal v_free_pb0, v_free_d, v_s_done
            (i_, v_a2_, pg_, a2b_) = p
            t1 = 0
            for k in range(4):
                t1 = tv(lambda e, k=k, a2b_=a2b_: e.transpose(ksl(PBbf[0], k),
                                                              a2b_[:, k * 128:(k + 1) * 128], IDN[:]),
                        waits=([("V", v_a2_), ("V", v_free_pb0)] if k == 0 else []))
            t_a2cons[i_ % 2] = t1
            v_a2f = vv(lambda e: e.tensor_copy(out=HQF[:, 0:512], in_=PBbf[0][:, 0:512]),
                       waits=[("T", t1)], mode="inst")
            v_free_pb0 = v_a2f
            td = 0
            for half in range(2):
                for k in range(4):
                    td = tv(lambda e, pg_=pg_, half=half, k=k: e.matmul(
                        PB[3 + half][:], ksl(HQF, k),
                        slot(2, pg_)[:, k * 1024 + half * 512:k * 1024 + half * 512 + 512],
                        start=(k == 0), stop=(k == 3)),
                        waits=([("V", v_a2f), ("V", v_free_d)] if (half == 0 and k == 0) else []))
            last = (pg_ == (NT2 - 1) % 2 and i_ == NCH - 1 and it == NT2 - 1)
            v_free_d = vv(lambda e, i_=i_: [e.tensor_tensor(out=ksl(XN, i_, D)[:, 0:512],
                                                            in0=ksl(XN, i_, D)[:, 0:512],
                                                            in1=PB[3][:], op=ALU.add),
                                            e.tensor_tensor(out=ksl(XN, i_, D)[:, 512:1024],
                                                            in0=ksl(XN, i_, D)[:, 512:1024],
                                                            in1=PB[4][:], op=ALU.add)][-1],
                          waits=[("T", td)] + ([("SD", dbg_x2_gate)] if (it == 0 and i_ == 0) else []),
                          mode="drain" if last else "inst")
            if last:
                v_s_done = v_free_d

        for i in range(NCH):
            par = i % 2
            gb, ub = (1, 2) if par == 0 else (5, 6)
            silb, a2b = (SGT[:, 0:512], SGT[:, 512:1024]) if par == 0 else (H1T[:, 0:512], H1T[:, 512:1024])
            tg = 0
            for k in range(NK):
                tg = tv(lambda e, pg=pg, i=i, k=k, gb=gb: e.matmul(
                    PB[gb][:], H2F[:, i * D + k * 128:i * D + (k + 1) * 128],
                    slot(0, pg)[:, k * 512:(k + 1) * 512], start=(k == 0), stop=(k == NK - 1)),
                    waits=([("W3A" if pg == 0 else "W3B", lw_t[it]), ("V", v_h2f[i]),
                            ("V", v_free_gp3[par])] if k == 0 else []))
            tu = 0
            for k in range(NK):
                tu = tv(lambda e, pg=pg, i=i, k=k, ub=ub: e.matmul(
                    PB[ub][:], H2F[:, i * D + k * 128:i * D + (k + 1) * 128],
                    slot(1, pg)[:, k * 512:(k + 1) * 512], start=(k == 0), stop=(k == NK - 1)))
            if pend3[0] is not None:
                emit_tail3(pend3[0])
                pend3[0] = None
            s_sg = sc(lambda e, gb=gb, silb=silb: e.activation(out=silb, in_=PB[gb][:], func=ACT.Silu),
                      waits=[("T", tu), ("V", v_free_gp3[par])], mode="drain")
            v_a2 = vv(lambda e, ub=ub, silb=silb, a2b=a2b: e.tensor_tensor(out=a2b, in0=silb, in1=PB[ub][:],
                                                                           op=ALU.mult),
                      waits=[("S", s_sg), ("T", t_a2cons[par])], mode="inst")
            v_free_gp3[par] = v_a2
            pend3[0] = (i, v_a2, pg, a2b)
            if i % 2 == 1:
                emit_wq(1)
        if pend3[0] is not None:
            emit_tail3(pend3[0])
            pend3[0] = None
        t_tile_done[pg] = kb.now("T")
    emit_wq(200)
    sy(lambda e: e.dma_start(out=ss_d.rearrange("(c p) n -> p c n", p=128),
                             in_=XN[:].rearrange("p (c n) -> p c n", c=NCH)),
       waits=[("V", v_s_done)], inc="SP")
    if "s" in dbg:
        sy(lambda e: e.dma_start(out=dbg_outs["s"].rearrange("(c p) n -> p c n", p=128),
                                 in_=XN[:].rearrange("p (c n) -> p c n", c=NCH)),
           waits=[("V", v_s_done)], inc="SD")
    st_spill_s = kb.now("SP")
    sd_spill_s = kb.now("SD")

    all_q_stores = kb.now("SQX")
    all_wq_stores = kb.now("SQ")
    if "WQ" in dbg:
        sy(lambda e: e.dma_start(out=dbg_outs["WQ"], in_=fgq_d), waits=[("SQ", all_q_stores)], inc="SD")
        sy(lambda e: e.dma_start(out=dbg_outs["WD"], in_=fdq_d), inc="SD")
        sy(lambda e: e.dma_start(out=dbg_outs["FU"], in_=fuq_d), inc="SD")
        sy(lambda e: e.dma_start(out=dbg_outs["XQ"], in_=xqs_d), inc="SD")
    v_xqf = [0] * NCH
    t_xq_tr = 0
    t_after_s3 = kb.now("T")
    for i in range(NCH):
        ldq = sy(lambda e, i=i: e.dma_start(out=H1T[:], in_=xqs_d[i * 128:(i + 1) * 128, :]),
                 waits=[("SQX", all_q_stores), ("T", t_after_s3)] if i == 0 else [("T", t_xq_tr)], inc="LH")
        t1 = 0
        for k in range(NK):
            t1 = tv(lambda e, k=k: e.transpose(ksl(PBbf[0], k), ksl(H1T, k), IDN[:]),
                    waits=([("LH", ldq), ("V", v_free_pb0), ("V", v_s_done)] if k == 0 else []))
        t_xq_tr = t1
        v_xqf[i] = vv(lambda e, i=i: e.tensor_copy(out=ksl(H2F, i, D), in_=PBbf[0][:]), waits=[("T", t1)], mode="inst")
        v_free_pb0 = v_xqf[i]

    IT3 = [(0, 512), (512, 512), (1024, 512), (1536, 512), (2048, 512), (2560, 256)]

    def slot4(s, pg):
        return WA[:, (pg * 3 + s) * 4096:(pg * 3 + s + 1) * 4096]

    t_after_s3 = kb.now("T")
    t_tile4_done = {0: t_after_s3, 1: t_after_s3}
    v_free_g = 0
    v_free_d = 0
    v_gacc = [0] * NCH
    WAf = WA[:].bitcast(F32)
    bufX = [WAf[:, 0:1024], WAf[:, 1024:2048]]
    bufS = [WAf[:, 2048:3072], WAf[:, 3072:4096]]
    bufT = [WAf[:, 4096:5120], WAf[:, 5120:6144]]
    s_g = [0] * NCH
    v_out = [0] * NCH

    def emit_s5(i):
        p = i % 2
        s_g[i] = sc(lambda e, i=i: e.activation(out=ksl(XN, i, D), in_=ksl(XN, i, D), func=ACT.Silu),
                    waits=[("V", v_gacc[i])], mode="drain")
        w_pre = ([("V", v_out[i - 2])] if i >= 2 else [("T", t_tile4_done[0]), ("V", qst["vqf"])])
        ldx = sy(lambda e, i=i, p=p: e.dma_start(out=bufX[p], in_=xo[i * 128:(i + 1) * 128, :]),
                 waits=w_pre, inc="LX%d" % p)
        lds = sy(lambda e, i=i, p=p: e.dma_start(out=bufS[p], in_=ss_d[i * 128:(i + 1) * 128, :]),
                 inc="LZ%d" % p)
        vv(lambda e, i=i, p=p: e.tensor_tensor(out=bufT[p], in0=bufS[p], in1=ksl(XN, i, D), op=ALU.mult),
           waits=[("LZ%d" % p, lds), ("S", s_g[i])])
        v_out[i] = vv(lambda e, i=i, p=p: e.tensor_tensor(out=bufT[p], in0=bufT[p], in1=bufX[p], op=ALU.add),
                      waits=[("LX%d" % p, ldx)], mode="drain")
        sy(lambda e, i=i, p=p: e.dma_start(out=out_d[i * 128:(i + 1) * 128, :], in_=bufT[p]),
           waits=[("V", v_out[i])], inc="ST")
    t_wtr = t_after_s3
    for ti, (i0, itw) in enumerate(IT3):
        pg = ti % 2
        nI = (itw + 127) // 128
        v_wslot = 0
        for wslot, src_q in ((0, fgq_d), (1, fuq_d)):
            for c in range(nI):
                ldw = sy(lambda e, src_q=src_q, i0=i0, c=c: e.dma_start(
                    out=WSTG[:], in_=src_q[i0 + c * 128:i0 + (c + 1) * 128, :]),
                    waits=[("T", t_wtr), ("T", t_tile4_done[pg]), ("SQ", all_wq_stores)], inc="LW4")
                if "WQ" in dbg and ti == 0 and wslot == 1 and c == nI - 1:
                    ws3g = sy(lambda e: e.dma_start(out=dbg_outs["WS3"], in_=WSTG[:, 0:D]),
                              waits=[("LW4", ldw)], inc="SD")
                    kb.ev("tensor", None, waits=[("SD", ws3g)], inc=None)
                t1 = 0
                for k in range(NK):
                    t1 = tv(lambda e, k=k: e.transpose(ksl(PBbf[0], k), ksl(WSTG, k), IDN[:]),
                            waits=([("LW4", ldw), ("V", v_free_pb0)] if k == 0 else []))
                t_wtr = t1
                v_wslot = vv(lambda e, wslot=wslot, pg=pg, c=c, itw=itw: [
                    e.tensor_copy(out=slot4(wslot, pg)[:, k * itw + c * 128:k * itw + (c + 1) * 128],
                                  in_=ksl(PBbf[0], k)) for k in range(NK)][-1],
                    waits=[("T", t1)], mode="drain")
                v_free_pb0 = v_wslot
        for hb in range(NK):
            ldw = sy(lambda e, hb=hb, i0=i0, itw=itw: e.dma_start(
                out=WSTG[:, 0:itw], in_=fdq_d[hb * 128:(hb + 1) * 128, i0:i0 + itw]),
                waits=[("T", t_wtr), ("SQ", all_wq_stores)], inc="LW4")
            t1 = 0
            for c in range(nI):
                t1 = tv(lambda e, c=c: e.transpose(ksl(PBbf[0], c), ksl(WSTG, c), IDN[:]),
                        waits=([("LW4", ldw), ("V", v_free_pb0)] if c == 0 else []))
            t_wtr = t1
            v_wslot = vv(lambda e, pg=pg, hb=hb, nI=nI: [
                e.tensor_copy(out=slot4(2, pg)[:, c * 1024 + hb * 128:c * 1024 + (hb + 1) * 128],
                              in_=ksl(PBbf[0], c)) for c in range(nI)][-1],
                waits=[("T", t1)], mode="drain")
            v_free_pb0 = v_wslot
        if "WQ" in dbg and ti == 0:
            sl0g = sy(lambda e: e.dma_start(out=dbg_outs["SL0"], in_=slot4(0, 0)), waits=[("V", v_wslot)], inc="SD")
            sl0g = sy(lambda e: e.dma_start(out=dbg_outs["SL1"], in_=slot4(1, 0)), inc="SD")
            kb.ev("tensor", None, waits=[("SD", sl0g)], inc=None)
        pend = None
        HTp = [HT[:, 0:512], O32[:, 0:512]]
        SGTp = [SGT[:, 0:512], SGT[:, 512:1024]]

        def emit_tail(p):
            nonlocal v_free_pb0, v_free_d
            (i0_, hq_vq, nI_, pg_, itw_) = p
            for pi, i in enumerate((i0_, i0_ + 1)):
                t1 = 0
                for c in range(nI_):
                    qo = pi * 512 + c * 128
                    t1 = tv(lambda e, c=c, qo=qo: e.transpose(ksl(PBbf[0], c), QOUT[:, qo:qo + 128], IDN[:]),
                            waits=([("V", hq_vq), ("V", v_free_pb0)] if c == 0 else []))
                qst["tcons"] = t1
                v_hqf = vv(lambda e, pi=pi, itw_=itw_: e.tensor_copy(out=HQF[:, pi * 512:pi * 512 + itw_],
                                                                     in_=PBbf[0][:, 0:itw_]),
                           waits=[("T", t1)], mode="inst")
                v_free_pb0 = v_hqf
                td = 0
                for half in range(2):
                    for c in range(nI_):
                        td = tv(lambda e, pg_=pg_, pi=pi, half=half, c=c: e.matmul(
                            PB[3 + half][:], HQF[:, pi * 512 + c * 128:pi * 512 + (c + 1) * 128],
                            slot4(2, pg_)[:, c * 1024 + half * 512:c * 1024 + half * 512 + 512],
                            start=(c == 0), stop=(c == nI_ - 1)),
                            waits=([("V", v_hqf), ("V", v_free_d)] if (half == 0 and c == 0) else []))
                if ti == 0:
                    v_free_d = vv(lambda e, i=i: [e.tensor_copy(out=ksl(XN, i, D)[:, 0:512], in_=PB[3][:]),
                                                  e.tensor_copy(out=ksl(XN, i, D)[:, 512:1024], in_=PB[4][:])][-1],
                                  waits=[("T", td)] + ([("SP", st_spill_s), ("SD", sd_spill_s)] if i == 0 else []),
                                  mode="inst")
                else:
                    v_free_d = vv(lambda e, i=i: [e.tensor_tensor(out=ksl(XN, i, D)[:, 0:512],
                                                                  in0=ksl(XN, i, D)[:, 0:512], in1=PB[3][:],
                                                                  op=ALU.add),
                                                  e.tensor_tensor(out=ksl(XN, i, D)[:, 512:1024],
                                                                  in0=ksl(XN, i, D)[:, 512:1024], in1=PB[4][:],
                                                                  op=ALU.add)][-1],
                                  waits=[("T", td)], mode="inst")
                if ti == len(IT3) - 1:
                    v_gacc[i] = v_free_d

        v_free_gp = [0, 0]
        for i0 in range(0, NCH, 2):
            v_hh = [0, 0]
            for i in (i0, i0 + 1):
                par = i % 2
                gb, ub = (1, 2) if par == 0 else (5, 6)
                tg = 0
                for k in range(NK):
                    tg = tv(lambda e, pg=pg, i=i, k=k, itw=itw, gb=gb: e.matmul(
                        PB[gb][:, 0:itw], H2F[:, i * D + k * 128:i * D + (k + 1) * 128],
                        slot4(0, pg)[:, k * itw:(k + 1) * itw], start=(k == 0), stop=(k == NK - 1)),
                        waits=([("V", v_wslot), ("V", v_xqf[i]), ("V", v_free_gp[par])] if k == 0 else []))
                tu = 0
                for k in range(NK):
                    tu = tv(lambda e, pg=pg, i=i, k=k, itw=itw, ub=ub: e.matmul(
                        PB[ub][:, 0:itw], H2F[:, i * D + k * 128:i * D + (k + 1) * 128],
                        slot4(1, pg)[:, k * itw:(k + 1) * itw], start=(k == 0), stop=(k == NK - 1)))
                if i == i0 and pend is not None:
                    done0 = pend[0]
                    emit_tail(pend)
                    pend = None
                    if ti == len(IT3) - 1:
                        emit_s5(done0)
                        emit_s5(done0 + 1)
                s_sg = sc(lambda e, itw=itw, gb=gb, par=par: e.activation(out=SGTp[par][:, 0:itw],
                                                                         in_=PB[gb][:, 0:itw], func=ACT.Silu),
                          waits=[("T", tu), ("V", v_free_gp[par])], mode="drain")
                v_h = vv(lambda e, itw=itw, ub=ub, par=par: e.tensor_tensor(
                    out=HTp[par][:, 0:itw], in0=SGTp[par][:, 0:itw], in1=PB[ub][:, 0:itw],
                    op=ALU.mult), waits=[("S", s_sg)], mode="inst")
                v_free_gp[par] = v_h
                v_hh[par] = v_h
            units = [dict(sb=HTp[0][:, 0:itw], vwait=v_hh[0], w=itw, slot=0),
                     dict(sb=HTp[1][:, 0:itw], vwait=v_hh[1], w=itw, slot=1)]
            hq_vq = emit_quant_units(units, to_sbuf=True)
            pend = (i0, hq_vq, nI, pg, itw)
        if pend is not None:
            done0 = pend[0]
            emit_tail(pend)
            pend = None
            if ti == len(IT3) - 1:
                emit_s5(done0)
                emit_s5(done0 + 1)
        t_tile4_done[pg] = kb.now("T")

    if "g" in dbg:
        sy(lambda e: e.dma_start(out=dbg_outs["g"].rearrange("(c p) n -> p c n", p=128),
                                 in_=XN[:].rearrange("p (c n) -> p c n", c=NCH)),
           waits=[("S", s_g[NCH - 1])], inc="SD")

    st_total = kb.now("ST")

    with (nc.Block() as block,):
        def emit_for(name):
            def body(eng):
                waited = {}
                for (waits, fn, inc, mode) in kb.evs[name]:
                    for (s, v) in waits:
                        if waited.get(s, 0) < v:
                            eng.wait_ge(sems[s], v)
                            waited[s] = v
                    inst = fn(eng) if fn is not None else None
                    if inc is not None:
                        if mode == "dma":
                            inst.then_inc(sems[inc], 16)
                        elif mode == "inst":
                            inst.then_inc(sems[inc], 1)
                        elif mode == "drain":
                            eng.drain().then_inc(sems[inc], 1)
                        elif mode == "nop":
                            eng.nop().then_inc(sems[inc], 1)
                if name == "sync":
                    eng.wait_ge(sems["ST"], st_total)
                    for s2 in ("SP", "SQ", "SD", "LDX", "LDP0", "LDP1", "LC", "LQ0", "LQ1",
                               "LW4", "LH", "LX0", "LX1", "LZ0", "LZ1", "SQX"):
                        if kb.now(s2):
                            eng.wait_ge(sems[s2], kb.now(s2))
            return body

        block.sync(emit_for("sync"))
        block.gpsimd(emit_for("gpsimd"))
        block.vector(emit_for("vector"))
        block.scalar(emit_for("scalar"))
        block.tensor(emit_for("tensor"))

    st_ctx.close()
    return nc, dbg_outs


_NC_CACHE = {}


def _get_program(dbg=()):
    key = tuple(sorted(dbg))
    if key not in _NC_CACHE:
        _NC_CACHE[key] = build_program(dbg)
    return _NC_CACHE[key]


def make_in_maps(x, wq, wk, wv, wo, anw, onw, mnw, wg, wu, wd, fgw, fuw, fdw):
    x = np.ascontiguousarray(np.asarray(x, dtype=np.float32))
    tri = np.triu(np.ones((128, 128), dtype=np.float32))
    idn = np.eye(128, dtype=ml_dtypes.bfloat16)
    anw_b = np.ascontiguousarray(np.broadcast_to(np.asarray(anw, np.float32), (128, D)))
    mnw_b = np.ascontiguousarray(np.broadcast_to(np.asarray(mnw, np.float32), (128, D)))
    onw_b = np.ascontiguousarray(np.broadcast_to(np.asarray(onw, np.float32), (128, 128)))
    shared = dict(
        wq=np.ascontiguousarray(wq, np.float32), wk=np.ascontiguousarray(wk, np.float32),
        wv=np.ascontiguousarray(wv, np.float32), wo=np.ascontiguousarray(wo, np.float32),
        wg=np.ascontiguousarray(wg, np.float32), wu=np.ascontiguousarray(wu, np.float32),
        wd=np.ascontiguousarray(wd, np.float32),
        fg=np.ascontiguousarray(fgw, np.float32), fu=np.ascontiguousarray(fuw, np.float32),
        fd=np.ascontiguousarray(fdw, np.float32),
        anw_b=anw_b, mnw_b=mnw_b, onw_b=onw_b, triu=tri, ident=idn,
    )
    zeros = np.zeros((R, D), dtype=np.float32)
    ones_m = np.ones((128, 1), dtype=np.float32)
    zeros_m = np.zeros((128, 1), dtype=np.float32)
    in_maps = []
    for c in range(8):
        b, half = c // 2, c % 2
        im = dict(shared)
        im["xo"] = np.ascontiguousarray(x[b, half * R:(half + 1) * R, :])
        im["xp"] = np.ascontiguousarray(x[b, 0:R, :]) if half == 1 else zeros
        im["pmask"] = ones_m if half == 1 else zeros_m
        in_maps.append(im)
    return in_maps


def kernel(L0_attn_hidden_states, L0_attn_attn_norm_w, L0_attn_Wq, L0_attn_Wk, L0_attn_Wv,
           L0_attn_Wo, L0_attn_attn_out_norm_w, L0_attn_mlp_norm_w, L0_attn_Wgate, L0_attn_Wup,
           L0_attn_Wdown, L0_ffn_w_gate, L0_ffn_w_up, L0_ffn_w_down):
    in_maps = make_in_maps(L0_attn_hidden_states, L0_attn_Wq, L0_attn_Wk, L0_attn_Wv, L0_attn_Wo,
                           L0_attn_attn_norm_w, L0_attn_attn_out_norm_w, L0_attn_mlp_norm_w,
                           L0_attn_Wgate, L0_attn_Wup, L0_attn_Wdown,
                           L0_ffn_w_gate, L0_ffn_w_up, L0_ffn_w_down)
    nc, _ = _get_program()
    res = run_bass_kernel_spmd(nc, in_maps, core_ids=list(range(8)))
    out = np.empty((4, 4096, D), dtype=np.float32)
    for c, r in enumerate(res.results):
        b, half = c // 2, c % 2
        out[b, half * R:(half + 1) * R, :] = r["out"]
    return out
